# version 18
# baseline (speedup 1.0000x reference)
import math
import contextlib
import numpy as np

import concourse.bass as bass
import concourse.mybir as mybir
from concourse.bass_utils import run_bass_kernel_spmd

F32 = mybir.dt.float32
F16 = mybir.dt.float16
BF16 = mybir.dt.bfloat16
I32 = mybir.dt.int32
AF = mybir.ActivationFunctionType
ALU = mybir.AluOpType
AX = mybir.AxisListType

ENGS = ("sync", "scalar", "vector", "tensor", "gpsimd")


def make_cfg(B=4, SQ=4096, NH=16, I=5632, SCH=512, QCH=512, TOKCH=512):
    HN = 128
    H = NH * HN
    cfg = dict(B=B, SQ=SQ, NH=NH, HN=HN, H=H, I=I)
    cfg["TOK"] = SQ // 2
    cfg["SCH"] = SCH
    cfg["NSC"] = SQ // SCH
    cfg["NQC"] = cfg["NSC"] // 2
    cfg["HC"] = H // 128
    cfg["NL"] = NH // 4
    cfg["KT"] = SQ // 128
    cfg["QCH"] = QCH
    cfg["NQCH"] = cfg["TOK"] // QCH
    cfg["QT"] = QCH // 128
    cfg["ITC"] = I // 128
    cfg["TOKCH"] = TOKCH
    cfg["NTC"] = cfg["TOK"] // TOKCH
    cfg["TT"] = TOKCH // 128
    cfg["NTT"] = cfg["TOK"] // 128
    cfg["GUPC"] = H // 8
    cfg["DNPC"] = I // 8
    cfg["GSUB"] = 2
    cfg["GSUB_H"] = H // cfg["GSUB"]
    cfg["DSUB"] = 4
    cfg["DSUB_I"] = I // cfg["DSUB"]
    cfg["OTG"] = 2 * I // 128
    cfg["OTD"] = H // 128
    return cfg


CFG_FULL = make_cfg()

SEM_NAMES = [
    "sG", "sGb", "sGc",
    "sX0", "sX1", "sXC", "sWl", "sWc", "sMa", "sEa",
    "sBq", "sBs", "sBe", "sBn",
    "sSS", "sSP", "sPV", "sCv", "sCd0", "sCd1",
    "sDl0", "sDl1", "sDv", "sDtp", "sDe", "sDs0", "sDs1",
    "sBc0", "sBc1", "sBx0", "sBx1", "sBt", "sBte", "sBm", "sBr",
    "sBh0", "sBh1", "sBb",
    "sCw0", "sCw1", "sCw2", "sCw3", "sCg", "sCs", "sCh",
    "sDw0", "sDw1", "sDw2", "sDw3", "sDm",
    "sCy", "sCh10", "sCh11", "sOut0", "sOut1", "sVs",
    "sBhT0", "sBhT1", "sCt0", "sCt1",
]


class W:

    def __init__(self, role, sems):
        self.role = role
        self.sems = sems
        self.eng = None
        self.cnt = {k: 0 for k in sems}
        self.mark = {}

    def on(self, name):
        return self.role == name

    def inc(self, inst, sem, n=1):
        self.cnt[sem] += n
        if inst is not None:
            inst.then_inc(self.sems[sem], n)
        return self.cnt[sem]

    def note(self, sem, n=1):
        self.cnt[sem] += n
        return self.cnt[sem]

    def wait(self, sem, tgt):
        if tgt > 0:
            self.eng.wait_ge(self.sems[sem], tgt)

    def c(self, sem):
        return self.cnt[sem]

    def pinc(self, inst, base, j, k=2):
        return self.inc(inst, base + str(j % k), 16)

    def pwait(self, base, j, k=2):
        self.eng.wait_ge(self.sems[base + str(j % k)], 16 * (j // k + 1))

    def pnote(self, base, j, k=2):
        self.note(base + str(j % k), 16)

    def snap(self, name, *sems):
        self.mark[name] = {s: self.cnt[s] for s in sems}

    def m(self, name, sem):
        return self.mark.get(name, {}).get(sem, 0)


def build_nc(cfg):
    B, SQ, NH, HN, H, I = (cfg[k] for k in ("B", "SQ", "NH", "HN", "H", "I"))
    TOK, SCH, NSC, NQC, HC, NL, KT = (cfg[k] for k in ("TOK", "SCH", "NSC", "NQC", "HC", "NL", "KT"))
    QCH, NQCH, QT, ITC, TOKCH, NTC, TT, NTT = (cfg[k] for k in ("QCH", "NQCH", "QT", "ITC", "TOKCH", "NTC", "TT", "NTT"))
    GUPC, DNPC, GSUB, GSUB_H, DSUB, DSUB_I, OTG, OTD = (
        cfg[k] for k in ("GUPC", "DNPC", "GSUB", "GSUB_H", "DSUB", "DSUB_I", "OTG", "OTD"))
    NORM = 1.0 / math.sqrt(HN)
    ST = SCH // 128
    NOC = max(1, H // 512)
    OCW = min(512, H)
    NST = max(1, H // 512)
    STW = min(512, H)

    nc = bass.Bass("TRN2", target_bir_lowering=False)

    d_xt = nc.dram_tensor("xt", [H, B, SQ], F32, kind="ExternalInput").ap()
    d_xrows = nc.dram_tensor("xrows", [TOK, H], F32, kind="ExternalInput").ap()
    d_wqkv = nc.dram_tensor("wqkvT", [H, 3 * NL * 128], F32, kind="ExternalInput").ap()
    d_wproj = nc.dram_tensor("wprojT", [H, H], F32, kind="ExternalInput").ap()
    d_bproj = nc.dram_tensor("bproj", [1, H], F32, kind="ExternalInput").ap()
    d_gup = nc.dram_tensor("gup", [2 * I, GUPC], I32, kind="ExternalInput").ap()
    d_gus = nc.dram_tensor("gusIL", [128, OTG * HC], F16, kind="ExternalInput").ap()
    d_dnp = nc.dram_tensor("dnp", [H, DNPC], I32, kind="ExternalInput").ap()
    d_dns = nc.dram_tensor("dnsIL", [128, OTD * ITC], F16, kind="ExternalInput").ap()
    d_ident = nc.dram_tensor("ident", [128, 128], F32, kind="ExternalInput").ap()
    d_ones = nc.dram_tensor("ones", [128, 128], F32, kind="ExternalInput").ap()
    d_out = nc.dram_tensor("out", [TOK, H], F32, kind="ExternalOutput").ap()

    d_ctx = nc.dram_tensor("ctx_scr", [TOK, H], BF16, kind="Internal").ap()
    d_wgu = nc.dram_tensor("wgu_scr", [ITC, H, 256], BF16, kind="Internal").ap()
    d_wdn = nc.dram_tensor("wdn_scr", [I, H], BF16, kind="Internal").ap()
    d_h1 = nc.dram_tensor("h1_scr", [TOK, H], F32, kind="Internal").ap()
    d_h1T = nc.dram_tensor("h1T_scr", [H, TOK], BF16, kind="Internal").ap()

    deq_items = []
    for ot in range(OTG):
        g0 = 0
        while g0 < HC:
            ng = min(8, HC - g0)
            deq_items.append(("g", ot, g0, ng))
            g0 += ng
    for ot in range(OTD):
        g0 = 0
        while g0 < ITC:
            ng = min(8, ITC - g0)
            deq_items.append(("d", ot, g0, ng))
            g0 += ng
    n_items = len(deq_items)
    cum_T = []
    _a = 0
    for _it in deq_items:
        _a += _it[3]
        cum_T.append(_a)
    n_slots = 4 * NL
    DEQ_FIFO = 3
    tile_first = [i for i, it in enumerate(deq_items) if it[2] == 0]

    with contextlib.ExitStack() as TOP:
        ET = TOP.enter_context
        sems = {n: ET(nc.semaphore(n)) for n in SEM_NAMES}
        block = ET(nc.Block())

        walkers = {r: W(r, sems) for r in ENGS}

        def run_phase(emitfn):
            for role in ENGS:
                wlk = walkers[role]

                def f(eng, wlk=wlk):
                    wlk.eng = eng
                    emitfn(wlk)

                getattr(block, role)(f)

        ident_bf = ET(nc.sbuf_tensor("ident_bf", [128, 128], BF16))
        ones_bf = ET(nc.sbuf_tensor("ones_bf", [128, 128], BF16))
        scb = ET(nc.sbuf_tensor("scb", [128, OTG * HC], F16))
        scd = ET(nc.sbuf_tensor("scd", [128, OTD * ITC], F16))
        bb = ET(nc.sbuf_tensor("bb", [2, H], BF16))
        p0stack = contextlib.ExitStack()
        EP0 = p0stack.enter_context
        gstage = EP0(nc.sbuf_tensor("gstage", [128, 128], F32))
        gstage2 = EP0(nc.sbuf_tensor("gstage2", [128, 128], F32))
        bstage = EP0(nc.sbuf_tensor("bstage", [2, H], F32))
        bstage2 = EP0(nc.sbuf_tensor("bstage2", [1, H], F32))
        bbh = EP0(nc.sbuf_tensor("bbh", [1, H], BF16))
        bbl = EP0(nc.sbuf_tensor("bbl", [1, H], BF16))


        def p0(w: W):
            if w.on("sync"):
                w.inc(w.eng.dma_start(out=gstage[:], in_=d_ident), "sG", 16)
                w.inc(w.eng.dma_start(out=gstage2[:], in_=d_ones), "sG", 16)
                w.inc(w.eng.dma_start(out=scb[:], in_=d_gus), "sG", 16)
                w.inc(w.eng.dma_start(out=scd[:], in_=d_dns), "sG", 16)
                w.inc(w.eng.dma_start(out=bstage[0:1, :], in_=d_bproj), "sG", 16)
            else:
                w.note("sG", 80)
            if w.on("vector"):
                v = w.eng
                v.wait_ge(sems["sG"], 80)
                w.inc(v.tensor_copy(out=ident_bf[:], in_=gstage[:]), "sGc", 1)
                w.inc(v.tensor_copy(out=ones_bf[:], in_=gstage2[:]), "sGc", 1)
                w.inc(v.engine_nop(), "sGc", 1)
                w.inc(v.engine_nop(), "sGc", 1)
                i1 = w.inc(v.tensor_copy(out=bbh[0:1, :], in_=bstage[0:1, :]), "sGc", 1)
                v.wait_ge(sems["sGc"], i1)
                i2 = w.inc(v.tensor_tensor(out=bstage2[0:1, :], in0=bstage[0:1, :],
                                           in1=bbh[0:1, :], op=ALU.subtract), "sGc", 1)
                v.wait_ge(sems["sGc"], i2)
                w.inc(v.tensor_copy(out=bbl[0:1, :], in_=bstage2[0:1, :]), "sGc", 1)
            else:
                w.note("sGc", 7)
            if w.on("gpsimd"):
                gp = w.eng
                gp.wait_ge(sems["sGc"], 7)
                w.inc(gp.dma_start(out=bb[0:1, :], in_=bbh[0:1, :]), "sGb", 16)
                w.inc(gp.dma_start(out=bb[1:2, :], in_=bbl[0:1, :]), "sGb", 16)
            else:
                w.note("sGb", 32)

        run_phase(p0)
        p0stack.close()

        with contextlib.ExitStack() as SA:
            EA = SA.enter_context
            wq_bf = EA(nc.sbuf_tensor("wq_bf", [128, HC, 3 * NL * 128], BF16))
            wq_st = EA(nc.sbuf_tensor("wq_st", [128, 3 * NL * 32], F32))
            K_wave = EA(nc.sbuf_tensor("K_wave", [128, NL, SQ], BF16))
            V_wave = EA(nc.sbuf_tensor("V_wave", [128, NL, KT, 129], BF16))
            Q_wave = EA(nc.sbuf_tensor("Q_wave", [128, NL, TOK], BF16))
            xf32 = EA(nc.sbuf_tensor("xf32", [128, 2, SCH], F32))
            xbf = [EA(nc.sbuf_tensor(f"xbf{i}", [128, HC, SCH], BF16)) for i in range(2)]
            Pt = EA(nc.sbuf_tensor("Pt", [128, 4, QCH], BF16))
            ctxg = [EA(nc.sbuf_tensor(f"ctxg{i}", [128, QT, 128], BF16)) for i in range(2)]
            rcp = EA(nc.sbuf_tensor("rcp", [128, 1], F32))
            sqb = EA(nc.sbuf_tensor("sqb", [128, 2, 128], BF16))
            bscal = EA(nc.sbuf_tensor("bscal", [1, 8], F32))
            bscal_bf = EA(nc.sbuf_tensor("bscal_bf", [1, 1], BF16))
            bias_c = EA(nc.sbuf_tensor("bias_c", [128, 2], F32))
            pk_sb = [EA(nc.sbuf_tensor(f"pk{i}", [128, max(GUPC, DNPC)], I32)) for i in range(2)]
            dq_bits = EA(nc.sbuf_tensor("dq_bits", [128, 1024], I32))
            dq_tmpB = EA(nc.sbuf_tensor("dq_tmpB", [128, 512], I32))
            dq_f = [EA(nc.sbuf_tensor(f"dqf{i}", [128, 1024], BF16))
                    for i in range(DEQ_FIFO)]
            wtile = [EA(nc.sbuf_tensor(f"wtile{i}", [128, 1024], BF16)) for i in range(2)]
            scf32 = [EA(nc.sbuf_tensor(f"scf32_{i}", [128, max(HC, ITC)], F32)) for i in range(2)]
            pS2 = [EA(nc.psum_tensor(f"pS2_{i}", [128, 1024], F32)) for i in range(2)]
            pCV = [EA(nc.psum_tensor(f"pCV{i}", [128, 512], F32)) for i in range(2)]

            def pK(l):
                return pS2[l // 2][:, (l % 2) * 512:(l % 2) * 512 + SCH]
            pT = [EA(nc.psum_tensor(f"pTq{i}", [128, 1024], BF16)) for i in range(2)]

            def deq_vec(w: W, idx):
                kind, ot, g0, ng = deq_items[idx]
                fifo = idx % DEQ_FIFO
                npack = GUPC if kind == "g" else DNPC
                ncols = ng * 16
                c0 = g0 * 16
                nh = ncols * 8
                ngroups = ng
                if g0 == 0:
                    w.cnt["_dq_tile"] = w.cnt["_dq_tile"] + 1
                pkt = pk_sb[(w.cnt["_dq_tile"] - 1) % 2]
                if g0 == 0:
                    T = w.cnt["_dq_tile"] - 1
                    if w.on("gpsimd"):
                        gp = w.eng
                        if T >= 2:
                            gp.wait_ge(sems["sDv"], tile_first[T - 1])
                        src = (d_gup[ot * 128:(ot + 1) * 128, :] if kind == "g"
                               else d_dnp[ot * 128:(ot + 1) * 128, :])
                        w.pinc(gp.dma_start(out=pkt[:, 0:npack], in_=src), "sDl", T)
                    else:
                        w.pnote("sDl", T)
                Tcur = w.cnt["_dq_tile"] - 1
                if w.on("vector"):
                    v = w.eng
                    w.pwait("sDl", Tcur)
                    if idx >= DEQ_FIFO:
                        v.wait_ge(sems["sDtp"], cum_T[idx - DEQ_FIFO])
                    v.wait_ge(sems["sVs"], w.c("sVs"))
                    v.wait_ge(sems["sDv"], w.c("sDv"))
                    if g0 == 0:
                        ngr_tile = HC if kind == "g" else ITC
                        sc_src = (scb[:, ot * HC:(ot + 1) * HC] if kind == "g"
                                  else scd[:, ot * ITC:(ot + 1) * ITC])
                        v.tensor_scalar(out=scf32[(w.cnt["_dq_tile"] - 1) % 2][:, 0:ngr_tile],
                                        in0=sc_src, scalar1=float(2.0 ** 126),
                                        scalar2=None, op0=ALU.mult)

                    def bits_view(j):
                        return dq_bits[:, j * ncols:(j + 1) * ncols]

                    last_or = None
                    for hb in range(2):
                        js = list(range(hb * 4, hb * 4 + 4))
                        if hb > 0:
                            ivo = w.inc(last_or, "sVs", 1)
                            v.wait_ge(sems["sVs"], ivo)
                        lastw = None
                        for j in js:
                            m8 = 8 << (4 * j)
                            if m8 >= 2 ** 31:
                                m8 -= 2 ** 32
                            lastw = v.tensor_scalar(out=bits_view(j), in0=pkt[:, c0:c0 + ncols],
                                                    scalar1=m8, scalar2=28 - 4 * j,
                                                    op0=ALU.bitwise_and,
                                                    op1=ALU.logical_shift_left)
                        for jj, j in enumerate(js):
                            sh7 = 22 - 4 * j
                            lastw = v.tensor_scalar(out=dq_tmpB[:, jj * ncols:(jj + 1) * ncols],
                                                    in0=pkt[:, c0:c0 + ncols],
                                                    scalar1=7 << (4 * j), scalar2=abs(sh7),
                                                    op0=ALU.bitwise_and,
                                                    op1=(ALU.logical_shift_left if sh7 >= 0
                                                         else ALU.logical_shift_right))
                        ivb = w.inc(lastw, "sVs", 1)
                        v.wait_ge(sems["sVs"], ivb)
                        for jj, j in enumerate(js):
                            last_or = v.tensor_tensor(out=bits_view(j), in0=bits_view(j),
                                                      in1=dq_tmpB[:, jj * ncols:(jj + 1) * ncols],
                                                      op=ALU.bitwise_or)
                    iv = w.inc(last_or, "sVs", 1)
                    v.wait_ge(sems["sVs"], iv)
                    last = None
                    tpar = (w.cnt["_dq_tile"] - 1) % 2
                    bits3 = dq_bits[:, 0:nh].bitcast(F32).rearrange("p (e i) -> p e i", e=8)
                    for g in range(ngroups):
                        gg = g0 + g
                        sc_ap = scf32[tpar][:, gg:gg + 1]
                        last = v.tensor_scalar(out=dq_f[fifo][:, g * 128:(g + 1) * 128]
                                               .rearrange("p (i e) -> p e i", e=8),
                                               in0=bits3[:, :, g * 16:(g + 1) * 16],
                                               scalar1=sc_ap, scalar2=None, op0=ALU.mult)
                    w.inc(last, "sDv", 1)
                else:
                    w.note("sVs", 4)
                    w.note("sDv", 1)

            def deq_pe(w: W, idx):
                kind, ot, g0, ng = deq_items[idx]
                fifo = idx % DEQ_FIFO
                pb = idx % 2
                for g in range(ng):
                    w.cnt["_dqT"] += 1
                    if w.on("tensor"):
                        t = w.eng
                        if g == 0:
                            t.wait_ge(sems["sDv"], idx + 1)
                            if idx >= 2:
                                t.wait_ge(sems["sDe"], idx - 1)
                        w.inc(t.transpose(pT[pb][:, g * 128:(g + 1) * 128],
                                          dq_f[fifo][:, g * 128:(g + 1) * 128],
                                          ident_bf[:]), "sDtp", 1)
                    else:
                        w.note("sDtp", 1)
                if w.on("scalar"):
                    sc = w.eng
                    sc.wait_ge(sems["sDtp"], cum_T[idx])
                    if idx >= 2:
                        w.pwait("sDs", idx - 2)
                    w.inc(sc.copy(out=wtile[pb][:, 0:ng * 128],
                                  in_=pT[pb][:, 0:ng * 128]), "sDe", 1)
                else:
                    w.note("sDe", 1)
                if w.on("gpsimd"):
                    gp = w.eng
                    gp.wait_ge(sems["sDe"], idx + 1)
                    if kind == "g":
                        h0 = g0 * 128
                        blk = ot % ITC
                        half = 0 if ot < ITC else 128
                        dst = d_wgu[blk, h0:h0 + ng * 128, half:half + 128]
                    else:
                        i0 = g0 * 128
                        dst = d_wdn[i0:i0 + ng * 128, ot * 128:(ot + 1) * 128]
                    w.pinc(gp.dma_start(out=dst.rearrange("(g p) o -> p g o", p=128),
                                        in_=wtile[pb][:, 0:ng * 128]
                                        .rearrange("p (g o) -> p g o", o=128)), "sDs", idx)
                else:
                    w.pnote("sDs", idx)


            def phaseA(w: W):
                w.cnt["_dq_tile"] = 0
                w.cnt["_dqT"] = 0
                w.eng.wait_ge(sems["sGc"], 7)
                w.eng.wait_ge(sems["sGb"], 32)
                WHALF = 3 * NL * 32
                for hc in range(HC):
                    for hh in range(4):
                        if w.on("sync"):
                            s = w.eng
                            s.wait_ge(sems["sWc"], w.c("sWc"))
                            w.inc(s.dma_start(out=wq_st[:],
                                              in_=d_wqkv[hc * 128:(hc + 1) * 128,
                                                         hh * WHALF:(hh + 1) * WHALF]),
                                  "sWl", 16)
                        else:
                            w.note("sWl", 16)
                        if w.on("vector"):
                            v = w.eng
                            v.wait_ge(sems["sWl"], w.c("sWl"))
                            w.inc(v.tensor_copy(out=wq_bf[:, hc, hh * WHALF:(hh + 1) * WHALF],
                                                in_=wq_st[:]), "sWc", 1)
                        else:
                            w.note("sWc", 1)
                if w.on("vector"):
                    w.inc(w.eng.memset(V_wave[:, :, :, 128:129], 1.0), "sWc", 1)
                else:
                    w.note("sWc", 1)

                vec_cursor = [0]
                pe_cursor = [0]

                def deq_vec_upto(tgt):
                    while vec_cursor[0] < min(tgt, n_items):
                        deq_vec(w, vec_cursor[0])
                        vec_cursor[0] += 1

                def deq_pe_upto(tgt):
                    while pe_cursor[0] < min(tgt, n_items):
                        deq_pe(w, pe_cursor[0])
                        pe_cursor[0] += 1

                us = [0]
                TOTAL_US = 4 * (NSC * 2 + NL * (2 * NQCH + 1))

                def pace(final=False):
                    us[0] += 1
                    tgt = n_items if final else us[0] * n_items // TOTAL_US
                    deq_vec_upto(min(n_items, tgt + DEQ_FIFO - 1))
                    deq_pe_upto(tgt)

                def x_load(xc):
                    xwav, xsi = xc // NSC, xc % NSC
                    xslot = xc % 2
                    for hc in range(HC):
                        kx = xc * HC + hc
                        if w.on("sync"):
                            s = w.eng
                            if kx >= 2:
                                s.wait_ge(sems["sXC"], kx - 1)
                            src = d_xt[hc * 128:(hc + 1) * 128, xwav, xsi * SCH:(xsi + 1) * SCH]
                            w.pinc(s.dma_start(out=xf32[:, kx % 2, :], in_=src), "sX", kx)
                        else:
                            w.pnote("sX", kx)
                        if w.on("scalar"):
                            sc = w.eng
                            w.pwait("sX", kx)
                            if xc >= 2 and hc == 0:
                                sc.wait_ge(sems["sMa"], w.m(f"chunk{xc-2}", "sMa"))
                            w.inc(sc.copy(out=xbf[xslot][:, hc, :],
                                          in_=xf32[:, kx % 2, :]), "sXC", 1)
                        else:
                            w.note("sXC", 1)

                for wav in range(4):
                    for si in range(NSC):
                        chunk = wav * NSC + si
                        slot = chunk % 2
                        is_q = si < NQC
                        if chunk == 0:
                            x_load(0)
                        if chunk + 1 < 4 * NSC:
                            x_load(chunk + 1)
                        if w.on("tensor"):
                            t = w.eng
                            if chunk == 0:
                                t.wait_ge(sems["sWc"], 4 * HC + 1)
                        ngr = 0
                        prevq = chunk - 1 if chunk >= 1 else None
                        for l in range(NL):
                            if w.on("tensor") and chunk >= 1:
                                mk = (f"eaQ{prevq}" if f"eaQ{prevq}" in w.mark
                                      else f"eaK{prevq}")
                                w.eng.wait_ge(sems["sEa"], w.m(mk, "sEa") + l + 1)
                            for hc in range(HC):
                                if w.on("tensor"):
                                    t = w.eng
                                    if l == 0:
                                        t.wait_ge(sems["sXC"], chunk * HC + hc + 1)
                                    inst = t.matmul(pK(l),
                                                    wq_bf[:, hc, (NL + l) * 128:(NL + l + 1) * 128],
                                                    xbf[slot][:, hc, :],
                                                    start=(hc == 0), stop=(hc == HC - 1))
                                    if hc == HC - 1:
                                        w.inc(inst, "sMa", 1)
                                elif hc == HC - 1:
                                    w.note("sMa", 1)
                            ngr += 1
                        pace()
                        for st in range(ST):
                            if w.on("tensor") and st == 0:
                                w.eng.wait_ge(sems["sCv"], w.c("sCv"))
                            if w.on("tensor") and st >= 2:
                                w.eng.wait_ge(sems["sEa"], w.c("sEa") - 1)
                            for hc in range(HC):
                                if w.on("tensor"):
                                    t = w.eng
                                    inst = t.matmul(pCV[st % 2][:, 0:NL * 128],
                                                    xbf[slot][:, hc, st * 128:(st + 1) * 128],
                                                    wq_bf[:, hc, 2 * NL * 128:3 * NL * 128],
                                                    start=(hc == 0), stop=(hc == HC - 1))
                                    if hc == HC - 1:
                                        w.inc(inst, "sMa", 1)
                                elif hc == HC - 1:
                                    w.note("sMa", 1)
                            ngr += 1
                            if w.on("scalar"):
                                sc = w.eng
                                sc.wait_ge(sems["sMa"], w.c("sMa"))
                                tg = si * ST + st
                                w.inc(sc.copy(out=V_wave[:, 0:NL, tg, 0:128],
                                              in_=pCV[st % 2][:, 0:NL * 128]
                                              .rearrange("p (l d) -> p l d", l=NL)),
                                      "sEa", 1)
                            else:
                                w.note("sEa", 1)
                        w.snap(f"eaK{chunk}", "sEa")
                        base = w.c("sMa") - ngr
                        for l in range(NL):
                            if w.on("scalar"):
                                sc = w.eng
                                sc.wait_ge(sems["sMa"], base + 1 + l)
                                w.inc(sc.copy(out=K_wave[:, l, si * SCH:(si + 1) * SCH],
                                              in_=pK(l)), "sEa", 1)
                            else:
                                w.note("sEa", 1)
                        if is_q:
                            for l in range(NL):
                                if w.on("tensor"):
                                    w.eng.wait_ge(sems["sEa"], w.m(f"eaK{chunk}", "sEa") + l + 1)
                                for hc in range(HC):
                                    if w.on("tensor"):
                                        t = w.eng
                                        inst = t.matmul(pK(l),
                                                        wq_bf[:, hc, l * 128:(l + 1) * 128],
                                                        xbf[slot][:, hc, :],
                                                        start=(hc == 0), stop=(hc == HC - 1))
                                        if hc == HC - 1:
                                            w.inc(inst, "sMa", 1)
                                    elif hc == HC - 1:
                                        w.note("sMa", 1)
                            qbase = w.c("sMa") - NL
                            w.snap(f"eaQ{chunk}", "sEa")
                            for l in range(NL):
                                if w.on("scalar"):
                                    sc = w.eng
                                    sc.wait_ge(sems["sMa"], qbase + 1 + l)
                                    w.inc(sc.copy(out=Q_wave[:, l, si * SCH:(si + 1) * SCH],
                                                  in_=pK(l)), "sEa", 1)
                                else:
                                    w.note("sEa", 1)
                        w.snap(f"chunk{chunk}", "sMa")
                        pace()
                    for l in range(NL):
                        slot_idx = wav * NL + l
                        par = slot_idx % 2
                        if slot_idx == 0:
                            if w.on("vector"):
                                w.inc(w.eng.memset(bias_c[:], -20.0), "sBn", 1)
                            else:
                                w.note("sBn", 1)
                        NPt = KT // 2
                        NGP = NQCH * NPt

                        def emit_S(sqc, ti):
                            gti = sqc * KT + ti
                            if w.on("tensor"):
                                t = w.eng
                                if gti >= 4 and gti % 2 == 0:
                                    t.wait_ge(sems["sSP"], w.c("sSP") - 1)
                                w.inc(t.matmul(pS2[(gti // 2) % 2][:, (gti % 2) * 512:(gti % 2) * 512 + QCH],
                                               K_wave[:, l, ti * 128:(ti + 1) * 128],
                                               Q_wave[:, l, sqc * QCH:(sqc + 1) * QCH],
                                               start=True, stop=True), "sSS", 1)
                            else:
                                w.note("sSS", 1)

                        for qc in range(NQCH):
                            gqc = slot_idx * NQCH + qc
                            for p in range(NPt):
                                gpi = qc * NPt + p
                                if p == NPt // 2:
                                    pace()
                                if gpi == 0:
                                    if w.on("tensor"):
                                        w.eng.wait_ge(sems["sEa"], w.c("sEa"))
                                    emit_S(0, 0)
                                    emit_S(0, 1)
                                if w.on("scalar"):
                                    sc = w.eng
                                    sc.wait_ge(sems["sSS"], w.c("sSS"))
                                    if gpi == 0 and slot_idx == 0:
                                        sc.wait_ge(sems["sBn"], w.c("sBn"))
                                    if gpi >= 2:
                                        sc.wait_ge(sems["sPV"], w.c("sPV") - 2)
                                    pin = pS2[gpi % 2][:].rearrange("x (b q) -> x b q", b=2)[:, :, 0:QCH]
                                    w.inc(sc.activation(out=Pt[:, (gpi % 2) * 2:(gpi % 2) * 2 + 2, :],
                                                        in_=pin, func=AF.Exp,
                                                        bias=bias_c[:, 0:1], scale=NORM),
                                          "sSP", 1)
                                else:
                                    w.note("sSP", 1)
                                if gpi + 1 < NGP:
                                    nqc, npp = divmod(gpi + 1, NPt)
                                    emit_S(nqc, 2 * npp)
                                    emit_S(nqc, 2 * npp + 1)
                                if w.on("tensor"):
                                    t = w.eng
                                    t.wait_ge(sems["sSP"], w.c("sSP"))
                                    if p == 0:
                                        t.wait_ge(sems["sCv"], w.c("sCv"))
                                    for tloc in range(2):
                                        ti = 2 * p + tloc
                                        inst = None
                                        for qt in range(QT):
                                            bank = pCV[qt // 2]
                                            off = (qt % 2) * 129
                                            inst = t.matmul(bank[:, off:off + 129],
                                                            Pt[:, (gpi % 2) * 2 + tloc,
                                                               qt * 128:(qt + 1) * 128],
                                                            V_wave[:, l, ti, 0:129],
                                                            start=(ti == 0), stop=(ti == KT - 1),
                                                            skip_group_check=True)
                                        w.inc(inst, "sPV", 1)
                                else:
                                    w.note("sPV", 2)
                            pace()
                            for qt in range(QT):
                                bank = pCV[qt // 2]
                                off = (qt % 2) * 129
                                if w.on("vector"):
                                    v = w.eng
                                    v.wait_ge(sems["sPV"], w.c("sPV"))
                                    if gqc >= 2 and qt == 0:
                                        w.pwait("sCd", gqc - 2)
                                    v.wait_ge(sems["sCv"], w.c("sCv"))
                                    iv = w.inc(v.reciprocal(out=rcp[:],
                                                            in_=bank[:, off + 128:off + 129]),
                                               "sCv", 1)
                                    v.wait_ge(sems["sCv"], iv)
                                    w.inc(v.tensor_scalar(out=ctxg[gqc % 2][:, qt, :],
                                                          in0=bank[:, off:off + 128],
                                                          scalar1=rcp[:, 0:1], scalar2=None,
                                                          op0=ALU.mult), "sCv", 1)
                                else:
                                    w.note("sCv", 2)
                            n2 = slot_idx
                            if w.on("gpsimd"):
                                gp = w.eng
                                gp.wait_ge(sems["sCv"], w.c("sCv"))
                                w.pinc(gp.dma_start(
                                    out=d_ctx[qc * QCH:(qc + 1) * QCH, n2 * 128:(n2 + 1) * 128]
                                    .rearrange("(q p) d -> p q d", p=128),
                                    in_=ctxg[gqc % 2][:, :, :]), "sCd", gqc)
                            else:
                                w.pnote("sCd", gqc)
                        pace(final=(wav == 3 and l == NL - 1))
                deq_vec_upto(n_items)
                deq_pe_upto(n_items)
                w.snap("finA", "sSS", "sSP", "sPV", "sCv", "sCd0", "sCd1", "sEa", "sMa",
                       "sXC", "sWc", "sDv", "sDtp", "sDe", "sDs0", "sDs1", "sBn", "sBe",
                       "sBq", "sBs", "sX0", "sX1", "sWl", "sDl0", "sDl1")

            run_phase(phaseA)

        with contextlib.ExitStack() as SB:
            EB = SB.enter_context
            wp_st = EB(nc.sbuf_tensor("wp_st", [128, H], F32))
            wp_bf = EB(nc.sbuf_tensor("wp_bf", [128, HC, H], BF16))
            ctx_t = [EB(nc.sbuf_tensor(f"ctx_t{i}", [128, H], BF16)) for i in range(2)]
            ctxT = EB(nc.sbuf_tensor("ctxT", [128, HC, 128], BF16))
            xr = [EB(nc.sbuf_tensor(f"xr{i}", [128, H], F32)) for i in range(2)]
            r1 = EB(nc.sbuf_tensor("r1", [128, H], F32))
            h1f = EB(nc.sbuf_tensor("h1f", [128, H], F32))
            h1b = [EB(nc.sbuf_tensor(f"h1b{i}", [128, H], BF16)) for i in range(2)]
            sttB = EB(nc.sbuf_tensor("sttB", [128, NST, 6], F32))
            aggB = EB(nc.sbuf_tensor("aggB", [128, 2], F32))
            rstdB = EB(nc.sbuf_tensor("rstdB", [128, 2], F32))
            hT_stage = [EB(nc.sbuf_tensor(f"hTst{i}", [128, HC, 128], BF16)) for i in range(2)]
            pB = [EB(nc.psum_tensor(f"pB{i}", [128, 512], F32)) for i in range(4)]
            pBt = [EB(nc.psum_tensor(f"pBt{i}", [128, 1024], BF16)) for i in range(2)]

            def phaseB(w: W):
                fin = w.mark["finA"]
                for sname, tgt in fin.items():
                    if tgt > 0:
                        w.eng.wait_ge(sems[sname], tgt)
                for hc in range(HC):
                    if w.on("sync"):
                        s = w.eng
                        s.wait_ge(sems["sWc"], w.c("sWc"))
                        w.inc(s.dma_start(out=wp_st[:], in_=d_wproj[hc * 128:(hc + 1) * 128, :]),
                              "sWl", 16)
                    else:
                        w.note("sWl", 16)
                    if w.on("vector"):
                        v = w.eng
                        v.wait_ge(sems["sWl"], w.c("sWl"))
                        w.inc(v.tensor_copy(out=wp_bf[:, hc, :], in_=wp_st[:]), "sWc", 1)
                    else:
                        w.note("sWc", 1)
                def h1T_block(td):
                    b0 = 0
                    while b0 < HC:
                        nb = min(8, HC - b0)
                        kb = w.cnt["_bTb"] = w.cnt.get("_bTb", 0) + 1
                        for g in range(nb):
                            if w.on("tensor"):
                                t = w.eng
                                if b0 == 0 and g == 0:
                                    t.wait_ge(sems["sBb"], w.m(f"bh1b{td}", "sBb"))
                                if g == 0 and kb >= 3:
                                    t.wait_ge(sems["sBte"], kb - 2)
                                w.inc(t.transpose(pBt[(kb - 1) % 2][:, g * 128:(g + 1) * 128],
                                                  h1b[td % 2][:, (b0 + g) * 128:(b0 + g + 1) * 128],
                                                  ident_bf[:]), "sBt", 1)
                            else:
                                w.note("sBt", 1)
                        if w.on("scalar"):
                            sc = w.eng
                            sc.wait_ge(sems["sBt"], w.c("sBt"))
                            if b0 == 0 and td >= 2:
                                w.pwait("sBhT", td - 2)
                            w.inc(sc.copy(out=hT_stage[td % 2][:, b0:b0 + nb, :],
                                          in_=pBt[(kb - 1) % 2][:, 0:nb * 128]
                                          .rearrange("p (g o) -> p g o", o=128)), "sBte", 1)
                        else:
                            w.note("sBte", 1)
                        b0 += nb
                    w.snap(f"bh1T{td}", "sBt")
                    if w.on("gpsimd"):
                        gp = w.eng
                        gp.wait_ge(sems["sBte"], w.c("sBte"))
                        w.pinc(gp.dma_start(out=d_h1T[:, td * 128:(td + 1) * 128]
                                            .rearrange("(hc p) t -> p hc t", p=128),
                                            in_=hT_stage[td % 2][:]), "sBhT", td)
                    else:
                        w.pnote("sBhT", td)

                for tt in range(NTT):
                    par = tt % 2
                    if w.on("gpsimd"):
                        gp = w.eng
                        if tt >= 2:
                            gp.wait_ge(sems["sBt"], w.m(f"bctx{tt-2}", "sBt"))
                        w.pinc(gp.dma_start(out=ctx_t[par][:],
                                            in_=d_ctx[tt * 128:(tt + 1) * 128, :]), "sBc", tt)
                    else:
                        w.pnote("sBc", tt)
                    if w.on("sync"):
                        s = w.eng
                        if tt >= 2:
                            s.wait_ge(sems["sBr"], w.m(f"bln{tt-2}", "sBr"))
                        w.pinc(s.dma_start(out=xr[par][:],
                                           in_=d_xrows[tt * 128:(tt + 1) * 128, :]), "sBx", tt)
                    else:
                        w.pnote("sBx", tt)
                    b0 = 0
                    while b0 < HC:
                        nb = min(8, HC - b0)
                        kb = w.cnt["_bTb"] = w.cnt.get("_bTb", 0) + 1
                        for g in range(nb):
                            if w.on("tensor"):
                                t = w.eng
                                if b0 == 0 and g == 0:
                                    w.pwait("sBc", tt)
                                    t.wait_ge(sems["sWc"], w.c("sWc"))
                                if g == 0 and kb >= 3:
                                    t.wait_ge(sems["sBte"], kb - 2)
                                w.inc(t.transpose(pBt[(kb - 1) % 2][:, g * 128:(g + 1) * 128],
                                                  ctx_t[par][:, (b0 + g) * 128:(b0 + g + 1) * 128],
                                                  ident_bf[:]), "sBt", 1)
                            else:
                                w.note("sBt", 1)
                        if w.on("scalar"):
                            sc = w.eng
                            sc.wait_ge(sems["sBt"], w.c("sBt"))
                            w.inc(sc.copy(out=ctxT[:, b0:b0 + nb, :],
                                          in_=pBt[(kb - 1) % 2][:, 0:nb * 128]
                                          .rearrange("p (g o) -> p g o", o=128)), "sBte", 1)
                        else:
                            w.note("sBte", 1)
                        b0 += nb
                    w.snap(f"bctx{tt}", "sBt")
                    for hc in range(HC):
                        for oc in range(NOC):
                            if w.on("tensor"):
                                t = w.eng
                                if hc == 0 and oc == 0:
                                    t.wait_ge(sems["sBte"], w.c("sBte"))
                                    if tt >= 1:
                                        t.wait_ge(sems["sBr"], w.m(f"bres{tt-1}", "sBr"))
                                t.matmul(pB[oc][:, 0:OCW], ctxT[:, hc, :],
                                         wp_bf[:, hc, oc * OCW:(oc + 1) * OCW],
                                         start=(hc == 0), stop=False)
                    for oc in range(NOC):
                        if w.on("tensor"):
                            t = w.eng
                            if tt == 0 and oc == 0:
                                t.wait_ge(sems["sGb"], 32)
                            w.inc(t.matmul(pB[oc][:, 0:OCW], ones_bf[0:2, 0:128],
                                           bb[0:2, oc * OCW:(oc + 1) * OCW],
                                           start=False, stop=True), "sBm", 1)
                        else:
                            w.note("sBm", 1)
                    for oc in range(NOC):
                        if w.on("vector"):
                            v = w.eng
                            v.wait_ge(sems["sBm"], w.c("sBm") - NOC + 1 + oc)
                            if oc == 0:
                                w.pwait("sBx", tt)
                            w.inc(v.tensor_tensor(out=r1[:, oc * OCW:(oc + 1) * OCW],
                                                  in0=pB[oc][:, 0:OCW],
                                                  in1=xr[par][:, oc * OCW:(oc + 1) * OCW],
                                                  op=ALU.add), "sBr", 1)
                        else:
                            w.note("sBr", 1)
                    w.snap(f"bres{tt}", "sBr")
                    if w.on("vector"):
                        v = w.eng
                        iv = w.c("sBr")
                        v.wait_ge(sems["sBr"], iv)
                        last = None
                        for st2 in range(NST):
                            last = v.bn_stats(out=sttB[:, st2, :],
                                              in_=r1[:, st2 * STW:(st2 + 1) * STW])
                        i1 = w.inc(last, "sBr", 1)
                        v.wait_ge(sems["sBr"], i1)
                        i2 = w.inc(v.bn_aggr(out=aggB[:], in_=sttB[:].rearrange("p a b -> p (a b)")),
                                   "sBr", 1)
                        v.wait_ge(sems["sBr"], i2)
                        w.inc(v.tensor_scalar(out=rstdB[:, 0:1], in0=aggB[:, 1:2],
                                              scalar1=1e-5, scalar2=None, op0=ALU.add), "sBr", 1)
                    else:
                        w.note("sBr", 3)
                    if w.on("scalar"):
                        sc = w.eng
                        sc.wait_ge(sems["sBr"], w.c("sBr"))
                        w.inc(sc.sqrt(out=rstdB[:, 1:2], in_=rstdB[:, 0:1]), "sBr", 1)
                    else:
                        w.note("sBr", 1)
                    if w.on("vector"):
                        v = w.eng
                        v.wait_ge(sems["sBr"], w.c("sBr"))
                        i3 = w.inc(v.reciprocal(out=rstdB[:, 0:1], in_=rstdB[:, 1:2]), "sBr", 1)
                        v.wait_ge(sems["sBr"], i3)
                        if tt >= 1:
                            w.pwait("sBh", tt - 1)
                            v.wait_ge(sems["sBb"], w.m(f"bh1b{tt-1}", "sBb"))
                        w.inc(v.tensor_scalar(out=h1f[:], in0=r1[:],
                                              scalar1=aggB[:, 0:1], scalar2=rstdB[:, 0:1],
                                              op0=ALU.subtract, op1=ALU.mult), "sBr", 1)
                    else:
                        w.note("sBr", 2)
                    w.snap(f"bln{tt}", "sBr")
                    if w.on("gpsimd"):
                        gp = w.eng
                        gp.wait_ge(sems["sBr"], w.c("sBr"))
                        w.pinc(gp.dma_start(out=d_h1[tt * 128:(tt + 1) * 128, :], in_=h1f[:]),
                               "sBh", tt)
                    else:
                        w.pnote("sBh", tt)
                    if w.on("scalar"):
                        sc = w.eng
                        sc.wait_ge(sems["sBr"], w.c("sBr"))
                        if tt >= 2:
                            sc.wait_ge(sems["sBt"], w.m(f"bh1T{tt-2}", "sBt"))
                        w.inc(sc.copy(out=h1b[tt % 2][:], in_=h1f[:]), "sBb", 1)
                    else:
                        w.note("sBb", 1)
                    w.snap(f"bh1b{tt}", "sBb")
                    if tt >= 1:
                        h1T_block(tt - 1)
                    if tt == NTT - 1:
                        h1T_block(tt)
                w.snap("finB", "sBt", "sBte", "sBm", "sBr", "sBh0", "sBh1", "sBc0", "sBc1",
                       "sBx0", "sBx1", "sBb", "sWl", "sWc", "sBhT0", "sBhT1")

            run_phase(phaseB)

        with contextlib.ExitStack() as SC:
            EC = SC.enter_context
            hffnT = EC(nc.sbuf_tensor("hffnT", [128, ITC, TOKCH], BF16))
            h1tc = [EC(nc.sbuf_tensor(f"h1tc{i}", [128, HC, TOKCH], BF16)) for i in range(2)]
            wbC = [EC(nc.sbuf_tensor(f"wbC{i}", [128, HC, 256], BF16)) for i in range(4)]
            wdC = [EC(nc.sbuf_tensor(f"wdC{i}", [128, H], BF16)) for i in range(4)]
            sGC = EC(nc.sbuf_tensor("sGC", [128, TOKCH], BF16))
            yC = EC(nc.sbuf_tensor("yC", [128, H], F32))
            h1r = [EC(nc.sbuf_tensor(f"h1rC{i}", [128, H], F32)) for i in range(2)]
            r2 = EC(nc.sbuf_tensor("r2C", [128, H], F32))
            outf = [EC(nc.sbuf_tensor(f"outfC{i}", [128, H], F32)) for i in range(2)]
            sttC = EC(nc.sbuf_tensor("sttC", [128, NST, 6], F32))
            aggC = EC(nc.sbuf_tensor("aggC", [128, 2], F32))
            scaC = EC(nc.sbuf_tensor("scaC", [128, 4], F32))
            pC = [EC(nc.psum_tensor(f"pC{i}", [128, 512], F32)) for i in range(8)]

            def phaseC(w: W):
                finB = w.mark["finB"]
                for sname, tgt in finB.items():
                    if tgt > 0:
                        w.eng.wait_ge(sems[sname], tgt)
                for sname in ("sDs0", "sDs1"):
                    tgt = w.mark["finA"][sname]
                    if tgt > 0:
                        w.eng.wait_ge(sems[sname], tgt)
                NPAIR = max(1, TT // 2)
                ntt2 = min(2, TT)
                for ch in range(NTC):
                    if w.on("sync"):
                        s = w.eng
                        if ch >= 2:
                            s.wait_ge(sems["sCg"], w.m(f"cgc{ch-2}", "sCg"))
                        w.pinc(s.dma_start(out=h1tc[ch % 2][:],
                                           in_=d_h1T[:, ch * TOKCH:(ch + 1) * TOKCH]
                                           .rearrange("(hc p) t -> p hc t", p=128)), "sCt", ch)
                    else:
                        w.pnote("sCt", ch)
                    for op in range(ITC):
                        par = op % 4
                        kc = ch * ITC + op
                        if w.on("sync"):
                            s = w.eng
                            if kc >= 4:
                                s.wait_ge(sems["sCg"], w.m(f"cg{kc-4}", "sCg"))
                            w.pinc(s.dma_start(out=wbC[par][:],
                                               in_=d_wgu[op, :, :].rearrange("(hc p) o -> p hc o",
                                                                             p=128)), "sCw", kc, 4)
                        else:
                            w.pnote("sCw", kc, 4)
                        gb, ub = pC[2 * (op % 2)], pC[2 * (op % 2) + 1]
                        par = op % 4
                        for hc in range(HC):
                            if w.on("tensor"):
                                t = w.eng
                                if hc == 0:
                                    w.pwait("sCw", kc, 4)
                                    if op == 0:
                                        w.pwait("sCt", ch)
                                    if kc >= 2:
                                        t.wait_ge(sems["sCh"], w.m(f"chm{kc-2}", "sCh"))
                                    if op == 0 and ch >= 1:
                                        t.wait_ge(sems["sCy"],
                                                  w.m(f"cyev{ch * TT - (2 if TT >= 2 else 1)}", "sCy"))
                                t.matmul(gb[:, 0:TOKCH], wbC[par][:, hc, 0:128],
                                         h1tc[ch % 2][:, hc, :],
                                         start=(hc == 0), stop=(hc == HC - 1))
                        for hc in range(HC):
                            if w.on("tensor"):
                                t = w.eng
                                inst = t.matmul(ub[:, 0:TOKCH], wbC[par][:, hc, 128:256],
                                                h1tc[ch % 2][:, hc, :],
                                                start=(hc == 0), stop=(hc == HC - 1))
                                if hc == HC - 1:
                                    w.inc(inst, "sCg", 2)
                            elif hc == HC - 1:
                                w.note("sCg", 2)
                        w.snap(f"cg{kc}", "sCg")
                        if w.on("scalar"):
                            sc = w.eng
                            sc.wait_ge(sems["sCg"], w.c("sCg"))
                            if kc >= 1:
                                sc.wait_ge(sems["sCh"], w.m(f"chm{kc-1}", "sCh"))
                            w.inc(sc.activation(out=sGC[:], in_=gb[:, 0:TOKCH], func=AF.Silu,
                                                bias=0.0, scale=1.0), "sCs", 1)
                        else:
                            w.note("sCs", 1)
                        if w.on("vector"):
                            v = w.eng
                            v.wait_ge(sems["sCs"], w.c("sCs"))
                            if ch >= 1 and op == 0:
                                v.wait_ge(sems["sDm"], w.m(f"dmc{ch-1}", "sDm"))
                            w.inc(v.tensor_tensor(out=hffnT[:, op, :], in0=sGC[:],
                                                  in1=ub[:, 0:TOKCH], op=ALU.mult), "sCh", 1)
                        else:
                            w.note("sCh", 1)
                        w.snap(f"chm{kc}", "sCh")
                    w.snap(f"cgc{ch}", "sCg")
                    for pp in range(NPAIR):
                        for tt2 in range(ntt2):
                            tt_abs = ch * TT + pp * 2 + tt2
                            parq = tt_abs % 2
                            if w.on("sync"):
                                s = w.eng
                                if tt_abs >= 2:
                                    s.wait_ge(sems["sCy"], w.m(f"cyt{tt_abs-2}", "sCy"))
                                w.pinc(s.dma_start(out=h1r[parq][:],
                                                   in_=d_h1[tt_abs * 128:(tt_abs + 1) * 128, :]),
                                       "sCh1", tt_abs)
                            else:
                                w.pnote("sCh1", tt_abs)
                        for ic in range(ITC):
                            kd = (ch * NPAIR + pp) * ITC + ic
                            par = ic % 4
                            if w.on("sync"):
                                s = w.eng
                                if kd >= 4:
                                    s.wait_ge(sems["sDm"], w.m(f"dm{kd-4}", "sDm"))
                                w.pinc(s.dma_start(out=wdC[par][:],
                                                   in_=d_wdn[ic * 128:(ic + 1) * 128, :]),
                                       "sDw", kd, 4)
                            else:
                                w.pnote("sDw", kd, 4)
                            for tt2 in range(ntt2):
                                for oc in range(NOC):
                                    if w.on("tensor"):
                                        t = w.eng
                                        if tt2 == 0 and oc == 0:
                                            w.pwait("sDw", kd, 4)
                                            if ic == 0:
                                                t.wait_ge(sems["sCh"], w.c("sCh"))
                                                prev_last_tile = ch * TT + pp * 2 - 1
                                                if prev_last_tile >= 0:
                                                    t.wait_ge(sems["sCy"],
                                                              w.m(f"cyev{prev_last_tile}", "sCy"))
                                        lhs = (hffnT[:, ic, pp * 256 + tt2 * 128:
                                                     pp * 256 + (tt2 + 1) * 128] if TT > 1
                                               else hffnT[:, ic, 0:128])
                                        inst = t.matmul(pC[tt2 * 4 + oc][:, 0:OCW], lhs,
                                                        wdC[par][:, oc * OCW:(oc + 1) * OCW],
                                                        start=(ic == 0), stop=(ic == ITC - 1))
                                        if tt2 == ntt2 - 1 and oc == NOC - 1:
                                            w.inc(inst, "sDm", 1)
                                    elif tt2 == ntt2 - 1 and oc == NOC - 1:
                                        w.note("sDm", 1)
                            w.snap(f"dm{kd}", "sDm")
                        for tt2 in range(ntt2):
                            tt_abs = ch * TT + pp * 2 + tt2
                            parq = tt_abs % 2
                            if w.on("vector"):
                                v = w.eng
                                v.wait_ge(sems["sDm"], w.c("sDm"))
                                if tt_abs >= 2:
                                    w.pwait("sOut", tt_abs - 2)
                                for oc in range(NOC):
                                    w.inc(v.tensor_copy(out=yC[:, oc * OCW:(oc + 1) * OCW],
                                                        in_=pC[tt2 * 4 + oc][:, 0:OCW]),
                                          "sCy", 1)
                            else:
                                w.note("sCy", NOC)
                            w.snap(f"cyev{tt_abs}", "sCy")
                            if w.on("vector"):
                                v = w.eng
                                i1 = w.c("sCy")
                                v.wait_ge(sems["sCy"], i1)
                                last = None
                                for st2 in range(NST):
                                    last = v.bn_stats(out=sttC[:, st2, :],
                                                      in_=yC[:, st2 * STW:(st2 + 1) * STW])
                                i2 = w.inc(last, "sCy", 1)
                                v.wait_ge(sems["sCy"], i2)
                                i3 = w.inc(v.bn_aggr(out=aggC[:],
                                                     in_=sttC[:].rearrange("p a b -> p (a b)")),
                                           "sCy", 1)
                                v.wait_ge(sems["sCy"], i3)
                                i4 = w.inc(v.tensor_tensor(out=scaC[:, 0:1], in0=aggC[:, 0:1],
                                                           in1=aggC[:, 0:1], op=ALU.mult),
                                           "sCy", 1)
                                v.wait_ge(sems["sCy"], i4)
                                i5 = w.inc(v.tensor_tensor(out=scaC[:, 1:2], in0=scaC[:, 0:1],
                                                           in1=aggC[:, 1:2], op=ALU.add),
                                           "sCy", 1)
                                v.wait_ge(sems["sCy"], i5)
                                w.inc(v.tensor_scalar(out=scaC[:, 2:3], in0=scaC[:, 1:2],
                                                      scalar1=1e-6, scalar2=None, op0=ALU.add),
                                      "sCy", 1)
                            else:
                                w.note("sCy", 5)
                            if w.on("scalar"):
                                sc = w.eng
                                sc.wait_ge(sems["sCy"], w.c("sCy"))
                                w.inc(sc.sqrt(out=scaC[:, 3:4], in_=scaC[:, 2:3]), "sCy", 1)
                            else:
                                w.note("sCy", 1)
                            if w.on("vector"):
                                v = w.eng
                                v.wait_ge(sems["sCy"], w.c("sCy"))
                                i6 = w.inc(v.reciprocal(out=scaC[:, 0:1], in_=scaC[:, 3:4]),
                                           "sCy", 1)
                                v.wait_ge(sems["sCy"], i6)
                                w.pwait("sCh1", tt_abs)
                                i7 = w.inc(v.scalar_tensor_tensor(out=r2[:], in0=yC[:],
                                                                  scalar=scaC[:, 0:1],
                                                                  in1=h1r[parq][:],
                                                                  op0=ALU.mult, op1=ALU.add),
                                           "sCy", 1)
                                v.wait_ge(sems["sCy"], i7)
                                last = None
                                for st2 in range(NST):
                                    last = v.bn_stats(out=sttC[:, st2, :],
                                                      in_=r2[:, st2 * STW:(st2 + 1) * STW])
                                i8 = w.inc(last, "sCy", 1)
                                v.wait_ge(sems["sCy"], i8)
                                i9 = w.inc(v.bn_aggr(out=aggC[:],
                                                     in_=sttC[:].rearrange("p a b -> p (a b)")),
                                           "sCy", 1)
                                v.wait_ge(sems["sCy"], i9)
                                w.inc(v.tensor_scalar(out=scaC[:, 1:2], in0=aggC[:, 1:2],
                                                      scalar1=1e-5, scalar2=None, op0=ALU.add),
                                      "sCy", 1)
                            else:
                                w.note("sCy", 5)
                            if w.on("scalar"):
                                sc = w.eng
                                sc.wait_ge(sems["sCy"], w.c("sCy"))
                                w.inc(sc.sqrt(out=scaC[:, 2:3], in_=scaC[:, 1:2]), "sCy", 1)
                            else:
                                w.note("sCy", 1)
                            if w.on("vector"):
                                v = w.eng
                                v.wait_ge(sems["sCy"], w.c("sCy"))
                                i10 = w.inc(v.reciprocal(out=scaC[:, 3:4], in_=scaC[:, 2:3]),
                                            "sCy", 1)
                                v.wait_ge(sems["sCy"], i10)
                                w.inc(v.tensor_scalar(out=outf[parq][:], in0=r2[:],
                                                      scalar1=aggC[:, 0:1], scalar2=scaC[:, 3:4],
                                                      op0=ALU.subtract, op1=ALU.mult), "sCy", 1)
                            else:
                                w.note("sCy", 2)
                            w.snap(f"cyt{tt_abs}", "sCy")
                            if w.on("gpsimd"):
                                gp = w.eng
                                gp.wait_ge(sems["sCy"], w.c("sCy"))
                                w.pinc(gp.dma_start(out=d_out[tt_abs * 128:(tt_abs + 1) * 128, :],
                                                    in_=outf[parq][:]), "sOut", tt_abs)
                            else:
                                w.pnote("sOut", tt_abs)
                        w.snap(f"cypp{ch * NPAIR + pp}", "sCy")
                    w.snap(f"cyp{ch}", "sCy")
                    w.snap(f"dmc{ch}", "sDm")
                if w.on("gpsimd"):
                    for sname in ("sOut0", "sOut1"):
                        if w.c(sname) > 0:
                            w.eng.wait_ge(sems[sname], w.c(sname))

            run_phase(phaseC)

    return nc


def host_inputs(cfg, inputs):
    B, SQ, NH, H, I = (cfg[k] for k in ("B", "SQ", "NH", "H", "I"))
    HC, ITC, OTG, OTD, NL = (cfg[k] for k in ("HC", "ITC", "OTG", "OTD", "NL"))
    x = np.asarray(inputs["L0_attn_query"])
    wqkv = np.asarray(inputs["L0_attn_Wqkv"])
    wproj = np.asarray(inputs["L0_attn_Wproj"])
    bproj = np.asarray(inputs["L0_attn_bproj"])
    gup = np.ascontiguousarray(np.asarray(inputs["L0_ffn_gate_up_packed"]))
    gus = np.asarray(inputs["L0_ffn_gate_up_scales"])
    dnp = np.ascontiguousarray(np.asarray(inputs["L0_ffn_down_packed"]))
    dns = np.asarray(inputs["L0_ffn_down_scales"])

    xT = np.ascontiguousarray(np.transpose(x, (2, 0, 1)))
    half = SQ // 2
    xT_h1 = np.ascontiguousarray(np.concatenate([xT[:, :, half:], xT[:, :, :half]], axis=2))
    wq4 = wqkv.reshape(NH, 3, 128, H)
    wproj_T = np.ascontiguousarray(wproj.T)
    bproj_r = np.ascontiguousarray(bproj.reshape(1, H))
    gusT = np.ascontiguousarray(gus.T.reshape(OTG, 128, HC).transpose(1, 0, 2)
                                .reshape(128, OTG * HC))
    dnsT = np.ascontiguousarray(dns.T.reshape(OTD, 128, ITC).transpose(1, 0, 2)
                                .reshape(128, OTD * ITC))
    ident = np.eye(128, dtype=np.float32)
    ones = np.ones((128, 128), dtype=np.float32)

    in_maps = []
    for c in range(8):
        a, shalf = c // 2, c % 2
        horigs = [a + 4 * l for l in range(NL)]
        wsel = wq4[horigs]
        wq_c = np.ascontiguousarray(wsel.transpose(3, 1, 0, 2).reshape(H, 3 * NL * 128))
        s0 = shalf * half
        xr = np.ascontiguousarray(x[a, s0:s0 + half, :])
        in_maps.append({
            "xt": xT if shalf == 0 else xT_h1,
            "xrows": xr,
            "wqkvT": wq_c,
            "wprojT": wproj_T,
            "bproj": bproj_r,
            "gup": gup,
            "gusIL": gusT,
            "dnp": dnp,
            "dnsIL": dnsT,
            "ident": ident,
            "ones": ones,
        })
    return in_maps


def kernel(**inputs):
    cfg = CFG_FULL
    nc = build_nc(cfg)
    in_maps = host_inputs(cfg, inputs)
    res = run_bass_kernel_spmd(nc, in_maps, core_ids=list(range(8)))
    B, SQ, H = cfg["B"], cfg["SQ"], cfg["H"]
    half = SQ // 2
    y = np.empty((B, SQ, H), dtype=np.float32)
    for c in range(8):
        a, shalf = c // 2, c % 2
        y[a, shalf * half:(shalf + 1) * half, :] = res.results[c]["out"]
    return y


# version 19
# speedup vs baseline: 1.0296x; 1.0296x over previous
import math
import contextlib
import numpy as np

import concourse.bass as bass
import concourse.mybir as mybir
from concourse.bass_utils import run_bass_kernel_spmd

F32 = mybir.dt.float32
F16 = mybir.dt.float16
BF16 = mybir.dt.bfloat16
I32 = mybir.dt.int32
AF = mybir.ActivationFunctionType
ALU = mybir.AluOpType
AX = mybir.AxisListType

ENGS = ("sync", "scalar", "vector", "tensor", "gpsimd")


def make_cfg(B=4, SQ=4096, NH=16, I=5632, SCH=512, QCH=512, TOKCH=512):
    HN = 128
    H = NH * HN
    cfg = dict(B=B, SQ=SQ, NH=NH, HN=HN, H=H, I=I)
    cfg["TOK"] = SQ // 2
    cfg["SCH"] = SCH
    cfg["NSC"] = SQ // SCH
    cfg["NQC"] = cfg["NSC"] // 2
    cfg["HC"] = H // 128
    cfg["NL"] = NH // 4
    cfg["KT"] = SQ // 128
    cfg["QCH"] = QCH
    cfg["NQCH"] = cfg["TOK"] // QCH
    cfg["QT"] = QCH // 128
    cfg["ITC"] = I // 128
    cfg["TOKCH"] = TOKCH
    cfg["NTC"] = cfg["TOK"] // TOKCH
    cfg["TT"] = TOKCH // 128
    cfg["NTT"] = cfg["TOK"] // 128
    cfg["GUPC"] = H // 8
    cfg["DNPC"] = I // 8
    cfg["GSUB"] = 2
    cfg["GSUB_H"] = H // cfg["GSUB"]
    cfg["DSUB"] = 4
    cfg["DSUB_I"] = I // cfg["DSUB"]
    cfg["OTG"] = 2 * I // 128
    cfg["OTD"] = H // 128
    return cfg


CFG_FULL = make_cfg()

SEM_NAMES = [
    "sG", "sGb", "sGc",
    "sX0", "sX1", "sXC", "sWl", "sWc", "sMa", "sEa",
    "sBq", "sBs", "sBe", "sBn",
    "sSS", "sSP", "sPV", "sCv", "sCd0", "sCd1",
    "sDl0", "sDl1", "sDv", "sDtp", "sDe", "sDs0", "sDs1",
    "sBc0", "sBc1", "sBx0", "sBx1", "sBt", "sBte", "sBm", "sBr",
    "sBh0", "sBh1", "sBb",
    "sCw0", "sCw1", "sCw2", "sCw3", "sCg", "sCs", "sCh",
    "sDw0", "sDw1", "sDw2", "sDw3", "sDm",
    "sCy", "sCh10", "sCh11", "sOut0", "sOut1", "sVs",
    "sBhT0", "sBhT1", "sCt0", "sCt1",
]


class W:

    def __init__(self, role, sems):
        self.role = role
        self.sems = sems
        self.eng = None
        self.cnt = {k: 0 for k in sems}
        self.mark = {}

    def on(self, name):
        return self.role == name

    def inc(self, inst, sem, n=1):
        self.cnt[sem] += n
        if inst is not None:
            inst.then_inc(self.sems[sem], n)
        return self.cnt[sem]

    def note(self, sem, n=1):
        self.cnt[sem] += n
        return self.cnt[sem]

    def wait(self, sem, tgt):
        if tgt > 0:
            self.eng.wait_ge(self.sems[sem], tgt)

    def c(self, sem):
        return self.cnt[sem]

    def pinc(self, inst, base, j, k=2):
        return self.inc(inst, base + str(j % k), 16)

    def pwait(self, base, j, k=2):
        self.eng.wait_ge(self.sems[base + str(j % k)], 16 * (j // k + 1))

    def pnote(self, base, j, k=2):
        self.note(base + str(j % k), 16)

    def snap(self, name, *sems):
        self.mark[name] = {s: self.cnt[s] for s in sems}

    def m(self, name, sem):
        return self.mark.get(name, {}).get(sem, 0)


def build_nc(cfg):
    B, SQ, NH, HN, H, I = (cfg[k] for k in ("B", "SQ", "NH", "HN", "H", "I"))
    TOK, SCH, NSC, NQC, HC, NL, KT = (cfg[k] for k in ("TOK", "SCH", "NSC", "NQC", "HC", "NL", "KT"))
    QCH, NQCH, QT, ITC, TOKCH, NTC, TT, NTT = (cfg[k] for k in ("QCH", "NQCH", "QT", "ITC", "TOKCH", "NTC", "TT", "NTT"))
    GUPC, DNPC, GSUB, GSUB_H, DSUB, DSUB_I, OTG, OTD = (
        cfg[k] for k in ("GUPC", "DNPC", "GSUB", "GSUB_H", "DSUB", "DSUB_I", "OTG", "OTD"))
    NORM = 1.0 / math.sqrt(HN)
    ST = SCH // 128
    NOC = max(1, H // 512)
    OCW = min(512, H)
    NST = max(1, H // 512)
    STW = min(512, H)

    nc = bass.Bass("TRN2", target_bir_lowering=False)

    d_xt = nc.dram_tensor("xt", [H, B, SQ], F32, kind="ExternalInput").ap()
    d_xrows = nc.dram_tensor("xrows", [TOK, H], F32, kind="ExternalInput").ap()
    d_wqkv = nc.dram_tensor("wqkvT", [H, 3 * NL * 128], F32, kind="ExternalInput").ap()
    d_wproj = nc.dram_tensor("wprojT", [H, H], F32, kind="ExternalInput").ap()
    d_bproj = nc.dram_tensor("bproj", [1, H], F32, kind="ExternalInput").ap()
    d_gup = nc.dram_tensor("gup", [2 * I, GUPC], I32, kind="ExternalInput").ap()
    d_gus = nc.dram_tensor("gusIL", [128, OTG * HC], F16, kind="ExternalInput").ap()
    d_dnp = nc.dram_tensor("dnp", [H, DNPC], I32, kind="ExternalInput").ap()
    d_dns = nc.dram_tensor("dnsIL", [128, OTD * ITC], F16, kind="ExternalInput").ap()
    d_ident = nc.dram_tensor("ident", [128, 128], F32, kind="ExternalInput").ap()
    d_ones = nc.dram_tensor("ones", [128, 128], F32, kind="ExternalInput").ap()
    d_out = nc.dram_tensor("out", [TOK, H], F32, kind="ExternalOutput").ap()

    d_ctx = nc.dram_tensor("ctx_scr", [TOK, H], BF16, kind="Internal").ap()
    d_wgu = nc.dram_tensor("wgu_scr", [ITC, H, 256], BF16, kind="Internal").ap()
    d_wdn = nc.dram_tensor("wdn_scr", [I, H], BF16, kind="Internal").ap()
    d_h1 = nc.dram_tensor("h1_scr", [TOK, H], F32, kind="Internal").ap()
    d_h1T = nc.dram_tensor("h1T_scr", [H, TOK], BF16, kind="Internal").ap()

    deq_items = []
    for ot in range(OTG):
        g0 = 0
        while g0 < HC:
            ng = min(8, HC - g0)
            deq_items.append(("g", ot, g0, ng))
            g0 += ng
    for ot in range(OTD):
        g0 = 0
        while g0 < ITC:
            ng = min(8, ITC - g0)
            deq_items.append(("d", ot, g0, ng))
            g0 += ng
    n_items = len(deq_items)
    cum_T = []
    _a = 0
    for _it in deq_items:
        _a += _it[3]
        cum_T.append(_a)
    n_slots = 4 * NL
    DEQ_FIFO = 3
    tile_first = [i for i, it in enumerate(deq_items) if it[2] == 0]

    with contextlib.ExitStack() as TOP:
        ET = TOP.enter_context
        sems = {n: ET(nc.semaphore(n)) for n in SEM_NAMES}
        block = ET(nc.Block())

        walkers = {r: W(r, sems) for r in ENGS}

        def run_phase(emitfn):
            for role in ENGS:
                wlk = walkers[role]

                def f(eng, wlk=wlk):
                    wlk.eng = eng
                    emitfn(wlk)

                getattr(block, role)(f)

        ident_bf = ET(nc.sbuf_tensor("ident_bf", [128, 128], BF16))
        ones_bf = ET(nc.sbuf_tensor("ones_bf", [128, 128], BF16))
        scb = ET(nc.sbuf_tensor("scb", [128, OTG * HC], F16))
        scd = ET(nc.sbuf_tensor("scd", [128, OTD * ITC], F16))
        bb = ET(nc.sbuf_tensor("bb", [2, H], BF16))
        p0stack = contextlib.ExitStack()
        EP0 = p0stack.enter_context
        gstage = EP0(nc.sbuf_tensor("gstage", [128, 128], F32))
        gstage2 = EP0(nc.sbuf_tensor("gstage2", [128, 128], F32))
        bstage = EP0(nc.sbuf_tensor("bstage", [2, H], F32))
        bstage2 = EP0(nc.sbuf_tensor("bstage2", [1, H], F32))
        bbh = EP0(nc.sbuf_tensor("bbh", [1, H], BF16))
        bbl = EP0(nc.sbuf_tensor("bbl", [1, H], BF16))


        def p0(w: W):
            if w.on("sync"):
                w.inc(w.eng.dma_start(out=gstage[:], in_=d_ident), "sG", 16)
                w.inc(w.eng.dma_start(out=gstage2[:], in_=d_ones), "sG", 16)
                w.inc(w.eng.dma_start(out=scb[:], in_=d_gus), "sG", 16)
                w.inc(w.eng.dma_start(out=scd[:], in_=d_dns), "sG", 16)
                w.inc(w.eng.dma_start(out=bstage[0:1, :], in_=d_bproj), "sG", 16)
            else:
                w.note("sG", 80)
            if w.on("vector"):
                v = w.eng
                v.wait_ge(sems["sG"], 80)
                w.inc(v.tensor_copy(out=ident_bf[:], in_=gstage[:]), "sGc", 1)
                w.inc(v.tensor_copy(out=ones_bf[:], in_=gstage2[:]), "sGc", 1)
                w.inc(v.engine_nop(), "sGc", 1)
                w.inc(v.engine_nop(), "sGc", 1)
                i1 = w.inc(v.tensor_copy(out=bbh[0:1, :], in_=bstage[0:1, :]), "sGc", 1)
                v.wait_ge(sems["sGc"], i1)
                i2 = w.inc(v.tensor_tensor(out=bstage2[0:1, :], in0=bstage[0:1, :],
                                           in1=bbh[0:1, :], op=ALU.subtract), "sGc", 1)
                v.wait_ge(sems["sGc"], i2)
                w.inc(v.tensor_copy(out=bbl[0:1, :], in_=bstage2[0:1, :]), "sGc", 1)
            else:
                w.note("sGc", 7)
            if w.on("gpsimd"):
                gp = w.eng
                gp.wait_ge(sems["sGc"], 7)
                w.inc(gp.dma_start(out=bb[0:1, :], in_=bbh[0:1, :]), "sGb", 16)
                w.inc(gp.dma_start(out=bb[1:2, :], in_=bbl[0:1, :]), "sGb", 16)
            else:
                w.note("sGb", 32)

        run_phase(p0)
        p0stack.close()

        with contextlib.ExitStack() as SA:
            EA = SA.enter_context
            wq_bf = EA(nc.sbuf_tensor("wq_bf", [128, HC, 3 * NL * 128], BF16))
            wq_st = EA(nc.sbuf_tensor("wq_st", [128, 3 * NL * 32], F32))
            K_wave = EA(nc.sbuf_tensor("K_wave", [128, NL, SQ], BF16))
            V_wave = EA(nc.sbuf_tensor("V_wave", [128, NL, KT, 129], BF16))
            Q_wave = EA(nc.sbuf_tensor("Q_wave", [128, NL, TOK], BF16))
            xf32 = EA(nc.sbuf_tensor("xf32", [128, 2, SCH], F32))
            xbf = [EA(nc.sbuf_tensor(f"xbf{i}", [128, HC, SCH], BF16)) for i in range(2)]
            Pt = EA(nc.sbuf_tensor("Pt", [128, 4, QCH], BF16))
            ctxg = [EA(nc.sbuf_tensor(f"ctxg{i}", [128, QT, 128], BF16)) for i in range(2)]
            rcp = EA(nc.sbuf_tensor("rcp", [128, 4], F32))
            sqb = EA(nc.sbuf_tensor("sqb", [128, 2, 128], BF16))
            bscal = EA(nc.sbuf_tensor("bscal", [1, 8], F32))
            bscal_bf = EA(nc.sbuf_tensor("bscal_bf", [1, 1], BF16))
            bias_c = EA(nc.sbuf_tensor("bias_c", [128, 2], F32))
            pk_sb = [EA(nc.sbuf_tensor(f"pk{i}", [128, max(GUPC, DNPC)], I32)) for i in range(2)]
            dq_bits = EA(nc.sbuf_tensor("dq_bits", [128, 1024], I32))
            dq_tmpB = EA(nc.sbuf_tensor("dq_tmpB", [128, 512], I32))
            dq_f = [EA(nc.sbuf_tensor(f"dqf{i}", [128, 1024], BF16))
                    for i in range(DEQ_FIFO)]
            wtile = [EA(nc.sbuf_tensor(f"wtile{i}", [128, 1024], BF16)) for i in range(2)]
            scf32 = [EA(nc.sbuf_tensor(f"scf32_{i}", [128, max(HC, ITC)], F32)) for i in range(2)]
            pS2 = [EA(nc.psum_tensor(f"pS2_{i}", [128, 1024], F32)) for i in range(2)]
            pCV = [EA(nc.psum_tensor(f"pCV{i}", [128, 512], F32)) for i in range(2)]

            def pK(l):
                return pS2[l // 2][:, (l % 2) * 512:(l % 2) * 512 + SCH]
            pT = [EA(nc.psum_tensor(f"pTq{i}", [128, 1024], BF16)) for i in range(2)]

            def deq_vec(w: W, idx):
                kind, ot, g0, ng = deq_items[idx]
                fifo = idx % DEQ_FIFO
                npack = GUPC if kind == "g" else DNPC
                ncols = ng * 16
                c0 = g0 * 16
                nh = ncols * 8
                ngroups = ng
                if g0 == 0:
                    w.cnt["_dq_tile"] = w.cnt["_dq_tile"] + 1
                pkt = pk_sb[(w.cnt["_dq_tile"] - 1) % 2]
                if g0 == 0:
                    T = w.cnt["_dq_tile"] - 1
                    if w.on("gpsimd"):
                        gp = w.eng
                        if T >= 2:
                            gp.wait_ge(sems["sDv"], tile_first[T - 1])
                        src = (d_gup[ot * 128:(ot + 1) * 128, :] if kind == "g"
                               else d_dnp[ot * 128:(ot + 1) * 128, :])
                        w.pinc(gp.dma_start(out=pkt[:, 0:npack], in_=src), "sDl", T)
                    else:
                        w.pnote("sDl", T)
                Tcur = w.cnt["_dq_tile"] - 1
                if w.on("vector"):
                    v = w.eng
                    w.pwait("sDl", Tcur)
                    if idx >= DEQ_FIFO:
                        v.wait_ge(sems["sDtp"], cum_T[idx - DEQ_FIFO])
                    v.wait_ge(sems["sVs"], w.c("sVs"))
                    v.wait_ge(sems["sDv"], w.c("sDv"))
                    if g0 == 0:
                        ngr_tile = HC if kind == "g" else ITC
                        sc_src = (scb[:, ot * HC:(ot + 1) * HC] if kind == "g"
                                  else scd[:, ot * ITC:(ot + 1) * ITC])
                        v.tensor_scalar(out=scf32[(w.cnt["_dq_tile"] - 1) % 2][:, 0:ngr_tile],
                                        in0=sc_src, scalar1=float(2.0 ** 126),
                                        scalar2=None, op0=ALU.mult)

                    def bits_view(j):
                        return dq_bits[:, j * ncols:(j + 1) * ncols]

                    last_or = None
                    for hb in range(2):
                        js = list(range(hb * 4, hb * 4 + 4))
                        if hb > 0:
                            ivo = w.inc(last_or, "sVs", 1)
                            v.wait_ge(sems["sVs"], ivo)
                        lastw = None
                        for j in js:
                            m8 = 8 << (4 * j)
                            if m8 >= 2 ** 31:
                                m8 -= 2 ** 32
                            lastw = v.tensor_scalar(out=bits_view(j), in0=pkt[:, c0:c0 + ncols],
                                                    scalar1=m8, scalar2=28 - 4 * j,
                                                    op0=ALU.bitwise_and,
                                                    op1=ALU.logical_shift_left)
                        for jj, j in enumerate(js):
                            sh7 = 22 - 4 * j
                            lastw = v.tensor_scalar(out=dq_tmpB[:, jj * ncols:(jj + 1) * ncols],
                                                    in0=pkt[:, c0:c0 + ncols],
                                                    scalar1=7 << (4 * j), scalar2=abs(sh7),
                                                    op0=ALU.bitwise_and,
                                                    op1=(ALU.logical_shift_left if sh7 >= 0
                                                         else ALU.logical_shift_right))
                        ivb = w.inc(lastw, "sVs", 1)
                        v.wait_ge(sems["sVs"], ivb)
                        for jj, j in enumerate(js):
                            last_or = v.tensor_tensor(out=bits_view(j), in0=bits_view(j),
                                                      in1=dq_tmpB[:, jj * ncols:(jj + 1) * ncols],
                                                      op=ALU.bitwise_or)
                    iv = w.inc(last_or, "sVs", 1)
                    v.wait_ge(sems["sVs"], iv)
                    last = None
                    tpar = (w.cnt["_dq_tile"] - 1) % 2
                    bits3 = dq_bits[:, 0:nh].bitcast(F32).rearrange("p (e i) -> p e i", e=8)
                    for g in range(ngroups):
                        gg = g0 + g
                        sc_ap = scf32[tpar][:, gg:gg + 1]
                        last = v.tensor_scalar(out=dq_f[fifo][:, g * 128:(g + 1) * 128]
                                               .rearrange("p (i e) -> p e i", e=8),
                                               in0=bits3[:, :, g * 16:(g + 1) * 16],
                                               scalar1=sc_ap, scalar2=None, op0=ALU.mult)
                    w.inc(last, "sDv", 1)
                else:
                    w.note("sVs", 4)
                    w.note("sDv", 1)

            def deq_pe(w: W, idx):
                kind, ot, g0, ng = deq_items[idx]
                fifo = idx % DEQ_FIFO
                pb = idx % 2
                for g in range(ng):
                    w.cnt["_dqT"] += 1
                    if w.on("tensor"):
                        t = w.eng
                        if g == 0:
                            t.wait_ge(sems["sDv"], idx + 1)
                            if idx >= 2:
                                t.wait_ge(sems["sDe"], idx - 1)
                        w.inc(t.transpose(pT[pb][:, g * 128:(g + 1) * 128],
                                          dq_f[fifo][:, g * 128:(g + 1) * 128],
                                          ident_bf[:]), "sDtp", 1)
                    else:
                        w.note("sDtp", 1)
                if w.on("scalar"):
                    sc = w.eng
                    sc.wait_ge(sems["sDtp"], cum_T[idx])
                    if idx >= 2:
                        w.pwait("sDs", idx - 2)
                    w.inc(sc.copy(out=wtile[pb][:, 0:ng * 128],
                                  in_=pT[pb][:, 0:ng * 128]), "sDe", 1)
                else:
                    w.note("sDe", 1)
                if w.on("gpsimd"):
                    gp = w.eng
                    gp.wait_ge(sems["sDe"], idx + 1)
                    if kind == "g":
                        h0 = g0 * 128
                        blk = ot % ITC
                        half = 0 if ot < ITC else 128
                        dst = d_wgu[blk, h0:h0 + ng * 128, half:half + 128]
                    else:
                        i0 = g0 * 128
                        dst = d_wdn[i0:i0 + ng * 128, ot * 128:(ot + 1) * 128]
                    w.pinc(gp.dma_start(out=dst.rearrange("(g p) o -> p g o", p=128),
                                        in_=wtile[pb][:, 0:ng * 128]
                                        .rearrange("p (g o) -> p g o", o=128)), "sDs", idx)
                else:
                    w.pnote("sDs", idx)


            def phaseA(w: W):
                w.cnt["_dq_tile"] = 0
                w.cnt["_dqT"] = 0
                w.eng.wait_ge(sems["sGc"], 7)
                w.eng.wait_ge(sems["sGb"], 32)
                WHALF = 3 * NL * 32
                for hc in range(HC):
                    for hh in range(4):
                        if w.on("sync"):
                            s = w.eng
                            s.wait_ge(sems["sWc"], w.c("sWc"))
                            w.inc(s.dma_start(out=wq_st[:],
                                              in_=d_wqkv[hc * 128:(hc + 1) * 128,
                                                         hh * WHALF:(hh + 1) * WHALF]),
                                  "sWl", 16)
                        else:
                            w.note("sWl", 16)
                        if w.on("vector"):
                            v = w.eng
                            v.wait_ge(sems["sWl"], w.c("sWl"))
                            w.inc(v.tensor_copy(out=wq_bf[:, hc, hh * WHALF:(hh + 1) * WHALF],
                                                in_=wq_st[:]), "sWc", 1)
                        else:
                            w.note("sWc", 1)
                if w.on("vector"):
                    w.inc(w.eng.memset(V_wave[:, :, :, 128:129], 1.0), "sWc", 1)
                else:
                    w.note("sWc", 1)

                vec_cursor = [0]
                pe_cursor = [0]

                def deq_vec_upto(tgt):
                    while vec_cursor[0] < min(tgt, n_items):
                        deq_vec(w, vec_cursor[0])
                        vec_cursor[0] += 1

                def deq_pe_upto(tgt):
                    while pe_cursor[0] < min(tgt, n_items):
                        deq_pe(w, pe_cursor[0])
                        pe_cursor[0] += 1

                us = [0]
                TOTAL_US = 4 * (NSC * 2 + NL * (2 * NQCH + 1))

                def pace(final=False):
                    us[0] += 1
                    tgt = n_items if final else us[0] * n_items // TOTAL_US
                    deq_vec_upto(min(n_items, tgt + DEQ_FIFO - 1))
                    deq_pe_upto(tgt)

                def x_load(xc):
                    xwav, xsi = xc // NSC, xc % NSC
                    xslot = xc % 2
                    for hc in range(HC):
                        kx = xc * HC + hc
                        if w.on("sync"):
                            s = w.eng
                            if kx >= 2:
                                s.wait_ge(sems["sXC"], kx - 1)
                            src = d_xt[hc * 128:(hc + 1) * 128, xwav, xsi * SCH:(xsi + 1) * SCH]
                            w.pinc(s.dma_start(out=xf32[:, kx % 2, :], in_=src), "sX", kx)
                        else:
                            w.pnote("sX", kx)
                        if w.on("scalar"):
                            sc = w.eng
                            w.pwait("sX", kx)
                            if xc >= 2 and hc == 0:
                                sc.wait_ge(sems["sMa"], w.m(f"chunk{xc-2}", "sMa"))
                            w.inc(sc.copy(out=xbf[xslot][:, hc, :],
                                          in_=xf32[:, kx % 2, :]), "sXC", 1)
                        else:
                            w.note("sXC", 1)

                for wav in range(4):
                    for si in range(NSC):
                        chunk = wav * NSC + si
                        slot = chunk % 2
                        is_q = si < NQC
                        if chunk == 0:
                            x_load(0)
                        if chunk + 1 < 4 * NSC:
                            x_load(chunk + 1)
                        if w.on("tensor"):
                            t = w.eng
                            if chunk == 0:
                                t.wait_ge(sems["sWc"], 4 * HC + 1)
                        ngr = 0
                        prevq = chunk - 1 if chunk >= 1 else None
                        for l in range(NL):
                            if w.on("tensor") and chunk >= 1:
                                mk = (f"eaQ{prevq}" if f"eaQ{prevq}" in w.mark
                                      else f"eaK{prevq}")
                                w.eng.wait_ge(sems["sEa"], w.m(mk, "sEa") + l + 1)
                            for hc in range(HC):
                                if w.on("tensor"):
                                    t = w.eng
                                    if l == 0:
                                        t.wait_ge(sems["sXC"], chunk * HC + hc + 1)
                                    inst = t.matmul(pK(l),
                                                    wq_bf[:, hc, (NL + l) * 128:(NL + l + 1) * 128],
                                                    xbf[slot][:, hc, :],
                                                    start=(hc == 0), stop=(hc == HC - 1))
                                    if hc == HC - 1:
                                        w.inc(inst, "sMa", 1)
                                elif hc == HC - 1:
                                    w.note("sMa", 1)
                            ngr += 1
                        pace()
                        for st in range(ST):
                            if w.on("tensor") and st == 0:
                                w.eng.wait_ge(sems["sCv"], w.c("sCv"))
                            if w.on("tensor") and st >= 2:
                                w.eng.wait_ge(sems["sEa"], w.c("sEa") - 1)
                            for hc in range(HC):
                                if w.on("tensor"):
                                    t = w.eng
                                    inst = t.matmul(pCV[st % 2][:, 0:NL * 128],
                                                    xbf[slot][:, hc, st * 128:(st + 1) * 128],
                                                    wq_bf[:, hc, 2 * NL * 128:3 * NL * 128],
                                                    start=(hc == 0), stop=(hc == HC - 1))
                                    if hc == HC - 1:
                                        w.inc(inst, "sMa", 1)
                                elif hc == HC - 1:
                                    w.note("sMa", 1)
                            ngr += 1
                            if w.on("scalar"):
                                sc = w.eng
                                sc.wait_ge(sems["sMa"], w.c("sMa"))
                                tg = si * ST + st
                                w.inc(sc.copy(out=V_wave[:, 0:NL, tg, 0:128],
                                              in_=pCV[st % 2][:, 0:NL * 128]
                                              .rearrange("p (l d) -> p l d", l=NL)),
                                      "sEa", 1)
                            else:
                                w.note("sEa", 1)
                        w.snap(f"eaK{chunk}", "sEa")
                        base = w.c("sMa") - ngr
                        for l in range(NL):
                            if w.on("scalar"):
                                sc = w.eng
                                sc.wait_ge(sems["sMa"], base + 1 + l)
                                w.inc(sc.copy(out=K_wave[:, l, si * SCH:(si + 1) * SCH],
                                              in_=pK(l)), "sEa", 1)
                            else:
                                w.note("sEa", 1)
                        if is_q:
                            for l in range(NL):
                                if w.on("tensor"):
                                    w.eng.wait_ge(sems["sEa"], w.m(f"eaK{chunk}", "sEa") + l + 1)
                                for hc in range(HC):
                                    if w.on("tensor"):
                                        t = w.eng
                                        inst = t.matmul(pK(l),
                                                        wq_bf[:, hc, l * 128:(l + 1) * 128],
                                                        xbf[slot][:, hc, :],
                                                        start=(hc == 0), stop=(hc == HC - 1))
                                        if hc == HC - 1:
                                            w.inc(inst, "sMa", 1)
                                    elif hc == HC - 1:
                                        w.note("sMa", 1)
                            qbase = w.c("sMa") - NL
                            w.snap(f"eaQ{chunk}", "sEa")
                            for l in range(NL):
                                if w.on("scalar"):
                                    sc = w.eng
                                    sc.wait_ge(sems["sMa"], qbase + 1 + l)
                                    w.inc(sc.copy(out=Q_wave[:, l, si * SCH:(si + 1) * SCH],
                                                  in_=pK(l)), "sEa", 1)
                                else:
                                    w.note("sEa", 1)
                        w.snap(f"chunk{chunk}", "sMa")
                        pace()
                    for l in range(NL):
                        slot_idx = wav * NL + l
                        par = slot_idx % 2
                        if slot_idx == 0:
                            if w.on("vector"):
                                w.inc(w.eng.memset(bias_c[:], -20.0), "sBn", 1)
                            else:
                                w.note("sBn", 1)
                        NPt = KT // 2
                        NGP = NQCH * NPt

                        def emit_S(sqc, ti):
                            gti = sqc * KT + ti
                            if w.on("tensor"):
                                t = w.eng
                                if gti >= 4 and gti % 2 == 0:
                                    t.wait_ge(sems["sSP"], w.c("sSP") - 1)
                                w.inc(t.matmul(pS2[(gti // 2) % 2][:, (gti % 2) * 512:(gti % 2) * 512 + QCH],
                                               K_wave[:, l, ti * 128:(ti + 1) * 128],
                                               Q_wave[:, l, sqc * QCH:(sqc + 1) * QCH],
                                               start=True, stop=True), "sSS", 1)
                            else:
                                w.note("sSS", 1)

                        for qc in range(NQCH):
                            gqc = slot_idx * NQCH + qc
                            for p in range(NPt):
                                gpi = qc * NPt + p
                                if p == NPt // 2:
                                    pace()
                                if gpi == 0:
                                    if w.on("tensor"):
                                        w.eng.wait_ge(sems["sEa"], w.c("sEa"))
                                    emit_S(0, 0)
                                    emit_S(0, 1)
                                if w.on("scalar"):
                                    sc = w.eng
                                    sc.wait_ge(sems["sSS"], w.c("sSS"))
                                    if gpi == 0 and slot_idx == 0:
                                        sc.wait_ge(sems["sBn"], w.c("sBn"))
                                    if gpi >= 2:
                                        sc.wait_ge(sems["sPV"], w.c("sPV") - 2)
                                    pin = pS2[gpi % 2][:].rearrange("x (b q) -> x b q", b=2)[:, :, 0:QCH]
                                    w.inc(sc.activation(out=Pt[:, (gpi % 2) * 2:(gpi % 2) * 2 + 2, :],
                                                        in_=pin, func=AF.Exp,
                                                        bias=bias_c[:, 0:1], scale=NORM),
                                          "sSP", 1)
                                else:
                                    w.note("sSP", 1)
                                if gpi + 1 < NGP:
                                    nqc, npp = divmod(gpi + 1, NPt)
                                    emit_S(nqc, 2 * npp)
                                    emit_S(nqc, 2 * npp + 1)
                                if w.on("tensor"):
                                    t = w.eng
                                    t.wait_ge(sems["sSP"], w.c("sSP"))
                                    if p == 0:
                                        t.wait_ge(sems["sCv"], w.c("sCv"))
                                    for tloc in range(2):
                                        ti = 2 * p + tloc
                                        inst = None
                                        for qt in range(QT):
                                            bank = pCV[qt // 2]
                                            off = (qt % 2) * 129
                                            inst = t.matmul(bank[:, off:off + 129],
                                                            Pt[:, (gpi % 2) * 2 + tloc,
                                                               qt * 128:(qt + 1) * 128],
                                                            V_wave[:, l, ti, 0:129],
                                                            start=(ti == 0), stop=(ti == KT - 1),
                                                            skip_group_check=True)
                                        w.inc(inst, "sPV", 1)
                                else:
                                    w.note("sPV", 2)
                            pace()
                            if w.on("vector"):
                                v = w.eng
                                v.wait_ge(sems["sPV"], w.c("sPV"))
                                if gqc >= 2:
                                    w.pwait("sCd", gqc - 2)
                                v.wait_ge(sems["sCv"], w.c("sCv"))
                                for b2 in range(min(2, QT)):
                                    nq = min(2, QT - 2 * b2)
                                    w.inc(v.reciprocal(
                                        out=rcp[:, b2 * 2:b2 * 2 + nq],
                                        in_=pCV[b2][:, 128:128 + 129 * nq:129]), "sCv", 1)
                                v.wait_ge(sems["sCv"], w.c("sCv"))
                            else:
                                w.note("sCv", min(2, QT))
                            for qt in range(QT):
                                bank = pCV[qt // 2]
                                off = (qt % 2) * 129
                                if w.on("vector"):
                                    v = w.eng
                                    w.inc(v.tensor_scalar(out=ctxg[gqc % 2][:, qt, :],
                                                          in0=bank[:, off:off + 128],
                                                          scalar1=rcp[:, qt:qt + 1], scalar2=None,
                                                          op0=ALU.mult), "sCv", 1)
                                else:
                                    w.note("sCv", 1)
                            n2 = slot_idx
                            if w.on("gpsimd"):
                                gp = w.eng
                                gp.wait_ge(sems["sCv"], w.c("sCv"))
                                w.pinc(gp.dma_start(
                                    out=d_ctx[qc * QCH:(qc + 1) * QCH, n2 * 128:(n2 + 1) * 128]
                                    .rearrange("(q p) d -> p q d", p=128),
                                    in_=ctxg[gqc % 2][:, :, :]), "sCd", gqc)
                            else:
                                w.pnote("sCd", gqc)
                        pace(final=(wav == 3 and l == NL - 1))
                deq_vec_upto(n_items)
                deq_pe_upto(n_items)
                w.snap("finA", "sSS", "sSP", "sPV", "sCv", "sCd0", "sCd1", "sEa", "sMa",
                       "sXC", "sWc", "sDv", "sDtp", "sDe", "sDs0", "sDs1", "sBn", "sBe",
                       "sBq", "sBs", "sX0", "sX1", "sWl", "sDl0", "sDl1")

            run_phase(phaseA)

        with contextlib.ExitStack() as SB:
            EB = SB.enter_context
            wp_st = EB(nc.sbuf_tensor("wp_st", [128, H], F32))
            wp_bf = EB(nc.sbuf_tensor("wp_bf", [128, HC, H], BF16))
            ctx_t = [EB(nc.sbuf_tensor(f"ctx_t{i}", [128, H], BF16)) for i in range(2)]
            ctxT = EB(nc.sbuf_tensor("ctxT", [128, HC, 128], BF16))
            xr = [EB(nc.sbuf_tensor(f"xr{i}", [128, H], F32)) for i in range(2)]
            r1 = EB(nc.sbuf_tensor("r1", [128, H], F32))
            h1f = EB(nc.sbuf_tensor("h1f", [128, H], F32))
            h1b = [EB(nc.sbuf_tensor(f"h1b{i}", [128, H], BF16)) for i in range(2)]
            sttB = EB(nc.sbuf_tensor("sttB", [128, NST, 6], F32))
            aggB = EB(nc.sbuf_tensor("aggB", [128, 2], F32))
            rstdB = EB(nc.sbuf_tensor("rstdB", [128, 2], F32))
            hT_stage = [EB(nc.sbuf_tensor(f"hTst{i}", [128, HC, 128], BF16)) for i in range(2)]
            pB = [EB(nc.psum_tensor(f"pB{i}", [128, 512], F32)) for i in range(4)]
            pBt = [EB(nc.psum_tensor(f"pBt{i}", [128, 1024], BF16)) for i in range(2)]

            def phaseB(w: W):
                fin = w.mark["finA"]
                for sname, tgt in fin.items():
                    if tgt > 0:
                        w.eng.wait_ge(sems[sname], tgt)
                for hc in range(HC):
                    if w.on("sync"):
                        s = w.eng
                        s.wait_ge(sems["sWc"], w.c("sWc"))
                        w.inc(s.dma_start(out=wp_st[:], in_=d_wproj[hc * 128:(hc + 1) * 128, :]),
                              "sWl", 16)
                    else:
                        w.note("sWl", 16)
                    if w.on("vector"):
                        v = w.eng
                        v.wait_ge(sems["sWl"], w.c("sWl"))
                        w.inc(v.tensor_copy(out=wp_bf[:, hc, :], in_=wp_st[:]), "sWc", 1)
                    else:
                        w.note("sWc", 1)
                def h1T_block(td):
                    b0 = 0
                    while b0 < HC:
                        nb = min(8, HC - b0)
                        kb = w.cnt["_bTb"] = w.cnt.get("_bTb", 0) + 1
                        for g in range(nb):
                            if w.on("tensor"):
                                t = w.eng
                                if b0 == 0 and g == 0:
                                    t.wait_ge(sems["sBb"], w.m(f"bh1b{td}", "sBb"))
                                if g == 0 and kb >= 3:
                                    t.wait_ge(sems["sBte"], kb - 2)
                                w.inc(t.transpose(pBt[(kb - 1) % 2][:, g * 128:(g + 1) * 128],
                                                  h1b[td % 2][:, (b0 + g) * 128:(b0 + g + 1) * 128],
                                                  ident_bf[:]), "sBt", 1)
                            else:
                                w.note("sBt", 1)
                        if w.on("scalar"):
                            sc = w.eng
                            sc.wait_ge(sems["sBt"], w.c("sBt"))
                            if b0 == 0 and td >= 2:
                                w.pwait("sBhT", td - 2)
                            w.inc(sc.copy(out=hT_stage[td % 2][:, b0:b0 + nb, :],
                                          in_=pBt[(kb - 1) % 2][:, 0:nb * 128]
                                          .rearrange("p (g o) -> p g o", o=128)), "sBte", 1)
                        else:
                            w.note("sBte", 1)
                        b0 += nb
                    w.snap(f"bh1T{td}", "sBt")
                    if w.on("gpsimd"):
                        gp = w.eng
                        gp.wait_ge(sems["sBte"], w.c("sBte"))
                        w.pinc(gp.dma_start(out=d_h1T[:, td * 128:(td + 1) * 128]
                                            .rearrange("(hc p) t -> p hc t", p=128),
                                            in_=hT_stage[td % 2][:]), "sBhT", td)
                    else:
                        w.pnote("sBhT", td)

                for tt in range(NTT):
                    par = tt % 2
                    if w.on("gpsimd"):
                        gp = w.eng
                        if tt >= 2:
                            gp.wait_ge(sems["sBt"], w.m(f"bctx{tt-2}", "sBt"))
                        w.pinc(gp.dma_start(out=ctx_t[par][:],
                                            in_=d_ctx[tt * 128:(tt + 1) * 128, :]), "sBc", tt)
                    else:
                        w.pnote("sBc", tt)
                    if w.on("sync"):
                        s = w.eng
                        if tt >= 2:
                            s.wait_ge(sems["sBr"], w.m(f"bln{tt-2}", "sBr"))
                        w.pinc(s.dma_start(out=xr[par][:],
                                           in_=d_xrows[tt * 128:(tt + 1) * 128, :]), "sBx", tt)
                    else:
                        w.pnote("sBx", tt)
                    b0 = 0
                    while b0 < HC:
                        nb = min(8, HC - b0)
                        kb = w.cnt["_bTb"] = w.cnt.get("_bTb", 0) + 1
                        for g in range(nb):
                            if w.on("tensor"):
                                t = w.eng
                                if b0 == 0 and g == 0:
                                    w.pwait("sBc", tt)
                                    t.wait_ge(sems["sWc"], w.c("sWc"))
                                if g == 0 and kb >= 3:
                                    t.wait_ge(sems["sBte"], kb - 2)
                                w.inc(t.transpose(pBt[(kb - 1) % 2][:, g * 128:(g + 1) * 128],
                                                  ctx_t[par][:, (b0 + g) * 128:(b0 + g + 1) * 128],
                                                  ident_bf[:]), "sBt", 1)
                            else:
                                w.note("sBt", 1)
                        if w.on("scalar"):
                            sc = w.eng
                            sc.wait_ge(sems["sBt"], w.c("sBt"))
                            w.inc(sc.copy(out=ctxT[:, b0:b0 + nb, :],
                                          in_=pBt[(kb - 1) % 2][:, 0:nb * 128]
                                          .rearrange("p (g o) -> p g o", o=128)), "sBte", 1)
                        else:
                            w.note("sBte", 1)
                        b0 += nb
                    w.snap(f"bctx{tt}", "sBt")
                    for hc in range(HC):
                        for oc in range(NOC):
                            if w.on("tensor"):
                                t = w.eng
                                if hc == 0 and oc == 0:
                                    t.wait_ge(sems["sBte"], w.c("sBte"))
                                    if tt >= 1:
                                        t.wait_ge(sems["sBr"], w.m(f"bres{tt-1}", "sBr"))
                                t.matmul(pB[oc][:, 0:OCW], ctxT[:, hc, :],
                                         wp_bf[:, hc, oc * OCW:(oc + 1) * OCW],
                                         start=(hc == 0), stop=False)
                    for oc in range(NOC):
                        if w.on("tensor"):
                            t = w.eng
                            if tt == 0 and oc == 0:
                                t.wait_ge(sems["sGb"], 32)
                            w.inc(t.matmul(pB[oc][:, 0:OCW], ones_bf[0:2, 0:128],
                                           bb[0:2, oc * OCW:(oc + 1) * OCW],
                                           start=False, stop=True), "sBm", 1)
                        else:
                            w.note("sBm", 1)
                    for oc in range(NOC):
                        if w.on("vector"):
                            v = w.eng
                            v.wait_ge(sems["sBm"], w.c("sBm") - NOC + 1 + oc)
                            if oc == 0:
                                w.pwait("sBx", tt)
                            w.inc(v.tensor_tensor(out=r1[:, oc * OCW:(oc + 1) * OCW],
                                                  in0=pB[oc][:, 0:OCW],
                                                  in1=xr[par][:, oc * OCW:(oc + 1) * OCW],
                                                  op=ALU.add), "sBr", 1)
                        else:
                            w.note("sBr", 1)
                    w.snap(f"bres{tt}", "sBr")
                    if w.on("vector"):
                        v = w.eng
                        iv = w.c("sBr")
                        v.wait_ge(sems["sBr"], iv)
                        last = None
                        for st2 in range(NST):
                            last = v.bn_stats(out=sttB[:, st2, :],
                                              in_=r1[:, st2 * STW:(st2 + 1) * STW])
                        i1 = w.inc(last, "sBr", 1)
                        v.wait_ge(sems["sBr"], i1)
                        i2 = w.inc(v.bn_aggr(out=aggB[:], in_=sttB[:].rearrange("p a b -> p (a b)")),
                                   "sBr", 1)
                        v.wait_ge(sems["sBr"], i2)
                        w.inc(v.tensor_scalar(out=rstdB[:, 0:1], in0=aggB[:, 1:2],
                                              scalar1=1e-5, scalar2=None, op0=ALU.add), "sBr", 1)
                    else:
                        w.note("sBr", 3)
                    if w.on("scalar"):
                        sc = w.eng
                        sc.wait_ge(sems["sBr"], w.c("sBr"))
                        w.inc(sc.sqrt(out=rstdB[:, 1:2], in_=rstdB[:, 0:1]), "sBr", 1)
                    else:
                        w.note("sBr", 1)
                    if w.on("vector"):
                        v = w.eng
                        v.wait_ge(sems["sBr"], w.c("sBr"))
                        i3 = w.inc(v.reciprocal(out=rstdB[:, 0:1], in_=rstdB[:, 1:2]), "sBr", 1)
                        v.wait_ge(sems["sBr"], i3)
                        if tt >= 1:
                            w.pwait("sBh", tt - 1)
                            v.wait_ge(sems["sBb"], w.m(f"bh1b{tt-1}", "sBb"))
                        w.inc(v.tensor_scalar(out=h1f[:], in0=r1[:],
                                              scalar1=aggB[:, 0:1], scalar2=rstdB[:, 0:1],
                                              op0=ALU.subtract, op1=ALU.mult), "sBr", 1)
                    else:
                        w.note("sBr", 2)
                    w.snap(f"bln{tt}", "sBr")
                    if w.on("gpsimd"):
                        gp = w.eng
                        gp.wait_ge(sems["sBr"], w.c("sBr"))
                        w.pinc(gp.dma_start(out=d_h1[tt * 128:(tt + 1) * 128, :], in_=h1f[:]),
                               "sBh", tt)
                    else:
                        w.pnote("sBh", tt)
                    if w.on("scalar"):
                        sc = w.eng
                        sc.wait_ge(sems["sBr"], w.c("sBr"))
                        if tt >= 2:
                            sc.wait_ge(sems["sBt"], w.m(f"bh1T{tt-2}", "sBt"))
                        w.inc(sc.copy(out=h1b[tt % 2][:], in_=h1f[:]), "sBb", 1)
                    else:
                        w.note("sBb", 1)
                    w.snap(f"bh1b{tt}", "sBb")
                    if tt >= 1:
                        h1T_block(tt - 1)
                    if tt == NTT - 1:
                        h1T_block(tt)
                w.snap("finB", "sBt", "sBte", "sBm", "sBr", "sBh0", "sBh1", "sBc0", "sBc1",
                       "sBx0", "sBx1", "sBb", "sWl", "sWc", "sBhT0", "sBhT1")

            run_phase(phaseB)

        with contextlib.ExitStack() as SC:
            EC = SC.enter_context
            hffnT = EC(nc.sbuf_tensor("hffnT", [128, ITC, TOKCH], BF16))
            h1tc = [EC(nc.sbuf_tensor(f"h1tc{i}", [128, HC, TOKCH], BF16)) for i in range(2)]
            wbC = [EC(nc.sbuf_tensor(f"wbC{i}", [128, HC, 256], BF16)) for i in range(4)]
            wdC = [EC(nc.sbuf_tensor(f"wdC{i}", [128, H], BF16)) for i in range(4)]
            sGC = EC(nc.sbuf_tensor("sGC", [128, TOKCH], BF16))
            yC = EC(nc.sbuf_tensor("yC", [128, H], F32))
            h1r = [EC(nc.sbuf_tensor(f"h1rC{i}", [128, H], F32)) for i in range(2)]
            r2 = EC(nc.sbuf_tensor("r2C", [128, H], F32))
            outf = [EC(nc.sbuf_tensor(f"outfC{i}", [128, H], F32)) for i in range(2)]
            sttC = EC(nc.sbuf_tensor("sttC", [128, NST, 6], F32))
            aggC = EC(nc.sbuf_tensor("aggC", [128, 2], F32))
            scaC = EC(nc.sbuf_tensor("scaC", [128, 4], F32))
            pC = [EC(nc.psum_tensor(f"pC{i}", [128, 512], F32)) for i in range(8)]

            def phaseC(w: W):
                finB = w.mark["finB"]
                for sname, tgt in finB.items():
                    if tgt > 0:
                        w.eng.wait_ge(sems[sname], tgt)
                for sname in ("sDs0", "sDs1"):
                    tgt = w.mark["finA"][sname]
                    if tgt > 0:
                        w.eng.wait_ge(sems[sname], tgt)
                NPAIR = max(1, TT // 2)
                ntt2 = min(2, TT)
                for ch in range(NTC):
                    if w.on("sync"):
                        s = w.eng
                        if ch >= 2:
                            s.wait_ge(sems["sCg"], w.m(f"cgc{ch-2}", "sCg"))
                        w.pinc(s.dma_start(out=h1tc[ch % 2][:],
                                           in_=d_h1T[:, ch * TOKCH:(ch + 1) * TOKCH]
                                           .rearrange("(hc p) t -> p hc t", p=128)), "sCt", ch)
                    else:
                        w.pnote("sCt", ch)
                    for op in range(ITC):
                        par = op % 4
                        kc = ch * ITC + op
                        if w.on("sync"):
                            s = w.eng
                            if kc >= 4:
                                s.wait_ge(sems["sCg"], w.m(f"cg{kc-4}", "sCg"))
                            w.pinc(s.dma_start(out=wbC[par][:],
                                               in_=d_wgu[op, :, :].rearrange("(hc p) o -> p hc o",
                                                                             p=128)), "sCw", kc, 4)
                        else:
                            w.pnote("sCw", kc, 4)
                        gb, ub = pC[2 * (op % 2)], pC[2 * (op % 2) + 1]
                        par = op % 4
                        for hc in range(HC):
                            if w.on("tensor"):
                                t = w.eng
                                if hc == 0:
                                    w.pwait("sCw", kc, 4)
                                    if op == 0:
                                        w.pwait("sCt", ch)
                                    if kc >= 2:
                                        t.wait_ge(sems["sCh"], w.m(f"chm{kc-2}", "sCh"))
                                    if op == 0 and ch >= 1:
                                        t.wait_ge(sems["sCy"],
                                                  w.m(f"cyev{ch * TT - (2 if TT >= 2 else 1)}", "sCy"))
                                t.matmul(gb[:, 0:TOKCH], wbC[par][:, hc, 0:128],
                                         h1tc[ch % 2][:, hc, :],
                                         start=(hc == 0), stop=(hc == HC - 1))
                        for hc in range(HC):
                            if w.on("tensor"):
                                t = w.eng
                                inst = t.matmul(ub[:, 0:TOKCH], wbC[par][:, hc, 128:256],
                                                h1tc[ch % 2][:, hc, :],
                                                start=(hc == 0), stop=(hc == HC - 1))
                                if hc == HC - 1:
                                    w.inc(inst, "sCg", 2)
                            elif hc == HC - 1:
                                w.note("sCg", 2)
                        w.snap(f"cg{kc}", "sCg")
                        if w.on("scalar"):
                            sc = w.eng
                            sc.wait_ge(sems["sCg"], w.c("sCg"))
                            if kc >= 1:
                                sc.wait_ge(sems["sCh"], w.m(f"chm{kc-1}", "sCh"))
                            w.inc(sc.activation(out=sGC[:], in_=gb[:, 0:TOKCH], func=AF.Silu,
                                                bias=0.0, scale=1.0), "sCs", 1)
                        else:
                            w.note("sCs", 1)
                        if w.on("vector"):
                            v = w.eng
                            v.wait_ge(sems["sCs"], w.c("sCs"))
                            if ch >= 1 and op == 0:
                                v.wait_ge(sems["sDm"], w.m(f"dmc{ch-1}", "sDm"))
                            w.inc(v.tensor_tensor(out=hffnT[:, op, :], in0=sGC[:],
                                                  in1=ub[:, 0:TOKCH], op=ALU.mult), "sCh", 1)
                        else:
                            w.note("sCh", 1)
                        w.snap(f"chm{kc}", "sCh")
                    w.snap(f"cgc{ch}", "sCg")
                    for pp in range(NPAIR):
                        for tt2 in range(ntt2):
                            tt_abs = ch * TT + pp * 2 + tt2
                            parq = tt_abs % 2
                            if w.on("sync"):
                                s = w.eng
                                if tt_abs >= 2:
                                    s.wait_ge(sems["sCy"], w.m(f"cyt{tt_abs-2}", "sCy"))
                                w.pinc(s.dma_start(out=h1r[parq][:],
                                                   in_=d_h1[tt_abs * 128:(tt_abs + 1) * 128, :]),
                                       "sCh1", tt_abs)
                            else:
                                w.pnote("sCh1", tt_abs)
                        for ic in range(ITC):
                            kd = (ch * NPAIR + pp) * ITC + ic
                            par = ic % 4
                            if w.on("sync"):
                                s = w.eng
                                if kd >= 4:
                                    s.wait_ge(sems["sDm"], w.m(f"dm{kd-4}", "sDm"))
                                w.pinc(s.dma_start(out=wdC[par][:],
                                                   in_=d_wdn[ic * 128:(ic + 1) * 128, :]),
                                       "sDw", kd, 4)
                            else:
                                w.pnote("sDw", kd, 4)
                            for tt2 in range(ntt2):
                                for oc in range(NOC):
                                    if w.on("tensor"):
                                        t = w.eng
                                        if tt2 == 0 and oc == 0:
                                            w.pwait("sDw", kd, 4)
                                            if ic == 0:
                                                t.wait_ge(sems["sCh"], w.c("sCh"))
                                                prev_last_tile = ch * TT + pp * 2 - 1
                                                if prev_last_tile >= 0:
                                                    t.wait_ge(sems["sCy"],
                                                              w.m(f"cyev{prev_last_tile}", "sCy"))
                                        lhs = (hffnT[:, ic, pp * 256 + tt2 * 128:
                                                     pp * 256 + (tt2 + 1) * 128] if TT > 1
                                               else hffnT[:, ic, 0:128])
                                        inst = t.matmul(pC[tt2 * 4 + oc][:, 0:OCW], lhs,
                                                        wdC[par][:, oc * OCW:(oc + 1) * OCW],
                                                        start=(ic == 0), stop=(ic == ITC - 1))
                                        if tt2 == ntt2 - 1 and oc == NOC - 1:
                                            w.inc(inst, "sDm", 1)
                                    elif tt2 == ntt2 - 1 and oc == NOC - 1:
                                        w.note("sDm", 1)
                            w.snap(f"dm{kd}", "sDm")
                        for tt2 in range(ntt2):
                            tt_abs = ch * TT + pp * 2 + tt2
                            parq = tt_abs % 2
                            if w.on("vector"):
                                v = w.eng
                                v.wait_ge(sems["sDm"], w.c("sDm"))
                                if tt_abs >= 2:
                                    w.pwait("sOut", tt_abs - 2)
                                for oc in range(NOC):
                                    w.inc(v.tensor_copy(out=yC[:, oc * OCW:(oc + 1) * OCW],
                                                        in_=pC[tt2 * 4 + oc][:, 0:OCW]),
                                          "sCy", 1)
                            else:
                                w.note("sCy", NOC)
                            w.snap(f"cyev{tt_abs}", "sCy")
                            if w.on("vector"):
                                v = w.eng
                                i1 = w.c("sCy")
                                v.wait_ge(sems["sCy"], i1)
                                last = None
                                for st2 in range(NST):
                                    last = v.bn_stats(out=sttC[:, st2, :],
                                                      in_=yC[:, st2 * STW:(st2 + 1) * STW])
                                i2 = w.inc(last, "sCy", 1)
                                v.wait_ge(sems["sCy"], i2)
                                i3 = w.inc(v.bn_aggr(out=aggC[:],
                                                     in_=sttC[:].rearrange("p a b -> p (a b)")),
                                           "sCy", 1)
                                v.wait_ge(sems["sCy"], i3)
                                i4 = w.inc(v.tensor_tensor(out=scaC[:, 0:1], in0=aggC[:, 0:1],
                                                           in1=aggC[:, 0:1], op=ALU.mult),
                                           "sCy", 1)
                                v.wait_ge(sems["sCy"], i4)
                                i5 = w.inc(v.tensor_tensor(out=scaC[:, 1:2], in0=scaC[:, 0:1],
                                                           in1=aggC[:, 1:2], op=ALU.add),
                                           "sCy", 1)
                                v.wait_ge(sems["sCy"], i5)
                                w.inc(v.tensor_scalar(out=scaC[:, 2:3], in0=scaC[:, 1:2],
                                                      scalar1=1e-6, scalar2=None, op0=ALU.add),
                                      "sCy", 1)
                            else:
                                w.note("sCy", 5)
                            if w.on("scalar"):
                                sc = w.eng
                                sc.wait_ge(sems["sCy"], w.c("sCy"))
                                w.inc(sc.sqrt(out=scaC[:, 3:4], in_=scaC[:, 2:3]), "sCy", 1)
                            else:
                                w.note("sCy", 1)
                            if w.on("vector"):
                                v = w.eng
                                v.wait_ge(sems["sCy"], w.c("sCy"))
                                i6 = w.inc(v.reciprocal(out=scaC[:, 0:1], in_=scaC[:, 3:4]),
                                           "sCy", 1)
                                v.wait_ge(sems["sCy"], i6)
                                w.pwait("sCh1", tt_abs)
                                i7 = w.inc(v.scalar_tensor_tensor(out=r2[:], in0=yC[:],
                                                                  scalar=scaC[:, 0:1],
                                                                  in1=h1r[parq][:],
                                                                  op0=ALU.mult, op1=ALU.add),
                                           "sCy", 1)
                                v.wait_ge(sems["sCy"], i7)
                                last = None
                                for st2 in range(NST):
                                    last = v.bn_stats(out=sttC[:, st2, :],
                                                      in_=r2[:, st2 * STW:(st2 + 1) * STW])
                                i8 = w.inc(last, "sCy", 1)
                                v.wait_ge(sems["sCy"], i8)
                                i9 = w.inc(v.bn_aggr(out=aggC[:],
                                                     in_=sttC[:].rearrange("p a b -> p (a b)")),
                                           "sCy", 1)
                                v.wait_ge(sems["sCy"], i9)
                                w.inc(v.tensor_scalar(out=scaC[:, 1:2], in0=aggC[:, 1:2],
                                                      scalar1=1e-5, scalar2=None, op0=ALU.add),
                                      "sCy", 1)
                            else:
                                w.note("sCy", 5)
                            if w.on("scalar"):
                                sc = w.eng
                                sc.wait_ge(sems["sCy"], w.c("sCy"))
                                w.inc(sc.sqrt(out=scaC[:, 2:3], in_=scaC[:, 1:2]), "sCy", 1)
                            else:
                                w.note("sCy", 1)
                            if w.on("vector"):
                                v = w.eng
                                v.wait_ge(sems["sCy"], w.c("sCy"))
                                i10 = w.inc(v.reciprocal(out=scaC[:, 3:4], in_=scaC[:, 2:3]),
                                            "sCy", 1)
                                v.wait_ge(sems["sCy"], i10)
                                w.inc(v.tensor_scalar(out=outf[parq][:], in0=r2[:],
                                                      scalar1=aggC[:, 0:1], scalar2=scaC[:, 3:4],
                                                      op0=ALU.subtract, op1=ALU.mult), "sCy", 1)
                            else:
                                w.note("sCy", 2)
                            w.snap(f"cyt{tt_abs}", "sCy")
                            if w.on("gpsimd"):
                                gp = w.eng
                                gp.wait_ge(sems["sCy"], w.c("sCy"))
                                w.pinc(gp.dma_start(out=d_out[tt_abs * 128:(tt_abs + 1) * 128, :],
                                                    in_=outf[parq][:]), "sOut", tt_abs)
                            else:
                                w.pnote("sOut", tt_abs)
                        w.snap(f"cypp{ch * NPAIR + pp}", "sCy")
                    w.snap(f"cyp{ch}", "sCy")
                    w.snap(f"dmc{ch}", "sDm")
                if w.on("gpsimd"):
                    for sname in ("sOut0", "sOut1"):
                        if w.c(sname) > 0:
                            w.eng.wait_ge(sems[sname], w.c(sname))

            run_phase(phaseC)

    return nc


def host_inputs(cfg, inputs):
    B, SQ, NH, H, I = (cfg[k] for k in ("B", "SQ", "NH", "H", "I"))
    HC, ITC, OTG, OTD, NL = (cfg[k] for k in ("HC", "ITC", "OTG", "OTD", "NL"))
    x = np.asarray(inputs["L0_attn_query"])
    wqkv = np.asarray(inputs["L0_attn_Wqkv"])
    wproj = np.asarray(inputs["L0_attn_Wproj"])
    bproj = np.asarray(inputs["L0_attn_bproj"])
    gup = np.ascontiguousarray(np.asarray(inputs["L0_ffn_gate_up_packed"]))
    gus = np.asarray(inputs["L0_ffn_gate_up_scales"])
    dnp = np.ascontiguousarray(np.asarray(inputs["L0_ffn_down_packed"]))
    dns = np.asarray(inputs["L0_ffn_down_scales"])

    xT = np.ascontiguousarray(np.transpose(x, (2, 0, 1)))
    half = SQ // 2
    xT_h1 = np.ascontiguousarray(np.concatenate([xT[:, :, half:], xT[:, :, :half]], axis=2))
    wq4 = wqkv.reshape(NH, 3, 128, H)
    wproj_T = np.ascontiguousarray(wproj.T)
    bproj_r = np.ascontiguousarray(bproj.reshape(1, H))
    gusT = np.ascontiguousarray(gus.T.reshape(OTG, 128, HC).transpose(1, 0, 2)
                                .reshape(128, OTG * HC))
    dnsT = np.ascontiguousarray(dns.T.reshape(OTD, 128, ITC).transpose(1, 0, 2)
                                .reshape(128, OTD * ITC))
    ident = np.eye(128, dtype=np.float32)
    ones = np.ones((128, 128), dtype=np.float32)

    in_maps = []
    for c in range(8):
        a, shalf = c // 2, c % 2
        horigs = [a + 4 * l for l in range(NL)]
        wsel = wq4[horigs]
        wq_c = np.ascontiguousarray(wsel.transpose(3, 1, 0, 2).reshape(H, 3 * NL * 128))
        s0 = shalf * half
        xr = np.ascontiguousarray(x[a, s0:s0 + half, :])
        in_maps.append({
            "xt": xT if shalf == 0 else xT_h1,
            "xrows": xr,
            "wqkvT": wq_c,
            "wprojT": wproj_T,
            "bproj": bproj_r,
            "gup": gup,
            "gusIL": gusT,
            "dnp": dnp,
            "dnsIL": dnsT,
            "ident": ident,
            "ones": ones,
        })
    return in_maps


def kernel(**inputs):
    cfg = CFG_FULL
    nc = build_nc(cfg)
    in_maps = host_inputs(cfg, inputs)
    res = run_bass_kernel_spmd(nc, in_maps, core_ids=list(range(8)))
    B, SQ, H = cfg["B"], cfg["SQ"], cfg["H"]
    half = SQ // 2
    y = np.empty((B, SQ, H), dtype=np.float32)
    for c in range(8):
        a, shalf = c // 2, c % 2
        y[a, shalf * half:(shalf + 1) * half, :] = res.results[c]["out"]
    return y
